# version 02 (final)
# baseline (speedup 1.0000x reference)
import math
from contextlib import ExitStack

import numpy as np
import ml_dtypes

import concourse.bass as bass
import concourse.mybir as mybir
from concourse.bass_utils import run_bass_kernel_spmd

F32 = mybir.dt.float32
BF16 = mybir.dt.bfloat16
I32 = mybir.dt.int32
AF = mybir.ActivationFunctionType
ALU = mybir.AluOpType
AX = mybir.AxisListType.X

T = 2048
E = 1024
H = 8
D = 64
II = 2816
NT = T // 128
NQB = 4
KCH = E // 128
ICH = II // 128
NHEADS = NQB * H
LAMBDA_INIT = 0.8 - 0.6 * math.exp(-0.3 * 1)
C1 = 6291456.0
NEG_BIG = -1e30
P40 = float(2.0 ** 40)
IBLK = [512, 512, 512, 512, 512, 256]
IB_OFF = [0, 512, 1024, 1536, 2048, 2560]
NIB = len(IBLK)
TOKGRP = 8
NGRP = NT // TOKGRP
NWT = 52
NWCHUNK = 60
WW = 1408

_DEBUG = set()


def quant_thunks(eng, src, dst, ws, F):
    nb = F // 16
    r3s = src[:, :F].rearrange("p (b k) -> p b k", k=16)
    r3d = dst[:, :F].rearrange("p (b k) -> p b k", k=16)
    r3t = ws["tmp"][:, :F].rearrange("p (b k) -> p b k", k=16)
    r3a = ws["a"][:, :F].rearrange("p (b k) -> p b k", k=16)
    th = []
    th.append((lambda: eng.tensor_reduce(
        ws["bm"][:, :nb], r3s, AX, ALU.max, apply_absolute_value=True), False))
    th.append((lambda: eng.tensor_scalar(
        ws["sgn"][:, :F].bitcast(I32), src[:, :F].bitcast(I32),
        -0x80000000, 0x3F800000, ALU.bitwise_and, ALU.bitwise_or), False))
    th.append((lambda: eng.tensor_scalar(
        ws["scale"][:, :nb], ws["bm"][:, :nb], 1.0 / 6.0, 1e-12,
        ALU.mult, ALU.max), True))
    th.append((lambda: eng.tensor_scalar(
        ws["a"][:, :F].bitcast(I32), src[:, :F].bitcast(I32),
        0x7FFFFFFF, None, ALU.bitwise_and), False))
    th.append((lambda: eng.reciprocal(ws["rs"][:, :nb], ws["scale"][:, :nb]),
               True))
    th.append((lambda: eng.tensor_tensor(
        r3a, r3a, ws["rs"][:, :nb, None].broadcast_to([128, nb, 16]),
        ALU.mult), True))
    th.append((lambda: eng.tensor_scalar(
        ws["lo"][:, :F], ws["a"][:, :F], 1.0, C1, ALU.min, ALU.add), True))
    th.append((lambda: eng.tensor_scalar(
        ws["hi"][:, :F], ws["a"][:, :F], 1.0, 6.0, ALU.max, ALU.min), False))
    th.append((lambda: eng.tensor_scalar(
        ws["lo"][:, :F], ws["lo"][:, :F], C1 + 1.0, None, ALU.subtract), True))
    th.append((lambda: eng.tensor_scalar(
        ws["hi"][:, :F].bitcast(I32), ws["hi"][:, :F].bitcast(I32),
        21, None, ALU.logical_shift_right), True))
    th.append((lambda: eng.tensor_scalar(
        ws["hi"][:, :F].bitcast(I32), ws["hi"][:, :F].bitcast(I32),
        1.0, None, ALU.add), True))
    th.append((lambda: eng.tensor_scalar(
        ws["hi"][:, :F].bitcast(I32), ws["hi"][:, :F].bitcast(I32),
        1, None, ALU.logical_shift_right), True))
    th.append((lambda: eng.tensor_scalar(
        ws["hi"][:, :F].bitcast(I32), ws["hi"][:, :F].bitcast(I32),
        22, None, ALU.logical_shift_left), True))
    th.append((lambda: eng.tensor_tensor(
        ws["tmp"][:, :F], ws["lo"][:, :F], ws["hi"][:, :F], ALU.add), True))
    th.append((lambda: eng.tensor_tensor(
        ws["tmp"][:, :F], ws["tmp"][:, :F], ws["sgn"][:, :F], ALU.mult), True))
    th.append((lambda: eng.tensor_tensor(
        r3d, r3t, ws["scale"][:, :nb, None].broadcast_to([128, nb, 16]),
        ALU.mult), True))
    return th


class VecEmitter:
    def __init__(self, eng, dummy):
        self.eng = eng
        self.dummy = dummy
        self.last = None

    def fence(self):
        self.eng.tensor_scalar(self.dummy[:, 0:2], self.dummy[:, 0:2],
                               1.0, None, ALU.mult)
        self.last = "_d"

    def emit(self, sid, thunk, needs_fence):
        if needs_fence == "w":
            return thunk()
        if needs_fence and self.last == sid:
            self.fence()
        r = thunk()
        self.last = sid
        return r

    def run(self, streams):
        idx = [0] * len(streams)
        while True:
            done = 0
            for si, (sid, ths) in enumerate(streams):
                if idx[si] >= len(ths):
                    done += 1
                    continue
                t, f = ths[idx[si]]
                self.emit(sid, t, f)
                idx[si] += 1
            if done == len(streams):
                break


def ns_of(qb):
    return 4 * qb + 4


def sc_base(qb, h):
    return sum(2 * ns_of(q) * H for q in range(qb)) + h * 2 * ns_of(qb)


def pv_base(qb, h):
    return sum(ns_of(q) * H for q in range(qb)) + h * ns_of(qb)


def mk_base(qb, h):
    return (qb * H + h) * 8


SC_TOT = sum(2 * ns_of(q) * H for q in range(NQB))
PV_TOT = sum(ns_of(q) * H for q in range(NQB))
BCB = 48
BC_PER = 80


def bc_base(hi):
    return BCB + hi * BC_PER


def build_program(debug=(), max_phase=4, p3_qb=NQB, p3_h=H, p3_stage=9):
    nc = bass.Bass("TRN2", target_bir_lowering=False)

    x_d = nc.dram_tensor("x", [T, E], F32, kind="ExternalInput")
    wq_d = nc.dram_tensor("wq", [E, E], F32, kind="ExternalInput")
    wk_d = nc.dram_tensor("wk", [E, E], F32, kind="ExternalInput")
    wv_d = nc.dram_tensor("wv", [E, E], F32, kind="ExternalInput")
    wo_d = nc.dram_tensor("wo", [E, E], F32, kind="ExternalInput")
    lq1_d = nc.dram_tensor("lq1", [D], F32, kind="ExternalInput")
    lk1_d = nc.dram_tensor("lk1", [D], F32, kind="ExternalInput")
    lq2_d = nc.dram_tensor("lq2", [D], F32, kind="ExternalInput")
    lk2_d = nc.dram_tensor("lk2", [D], F32, kind="ExternalInput")
    g_d = nc.dram_tensor("g", [128], F32, kind="ExternalInput")
    wg_d = nc.dram_tensor("wg", [II, E], F32, kind="ExternalInput")
    wu_d = nc.dram_tensor("wu", [II, E], F32, kind="ExternalInput")
    wd_d = nc.dram_tensor("wd", [E, II], F32, kind="ExternalInput")

    y_d = nc.dram_tensor("y", [T, E], F32, kind="ExternalOutput")

    wgq_s = nc.dram_tensor("wgq_s", [II, E], BF16, kind="Internal")
    wuq_s = nc.dram_tensor("wuq_s", [II, E], BF16, kind="Internal")
    wdq_s = nc.dram_tensor("wdq_s", [E, II], BF16, kind="Internal")
    ts_s = nc.dram_tensor("ts_s", [T, E], F32, kind="Internal")
    bc_s = nc.dram_tensor("bc_s", [8, 512], F32, kind="Internal")
    cc_s = nc.dram_tensor("cc_s", [32], F32, kind="Internal")

    dbg = {}
    ndbg = [0]

    def dbg_t(name, shape, dt=F32):
        if name in debug:
            dbg[name] = nc.dram_tensor("dbg_" + name, shape, dt,
                                       kind="ExternalOutput")
        return dbg.get(name)

    dbg_t("xnT", [128, KCH * T], BF16)
    dbg_t("xqT", [128, KCH * T], BF16)
    dbg_t("QT", [128, H * T], BF16)
    dbg_t("KT", [128, H * T], BF16)
    dbg_t("V", [128, NT * E], BF16)
    dbg_t("bias16", [128, 16], F32)
    dbg_t("tattn", [T, E], F32)
    dbg_t("wgq", [II, E], BF16)
    dbg_t("wuq", [II, E], BF16)
    dbg_t("wdq", [E, II], BF16)
    dbg_t("lam", [1, 8], F32)
    dbg_t("p3trace", [NHEADS, 5 * 512], F32)
    dbg_t("arch", [128, 2 * 512], F32)
    dbg_t("aTd", [128, H * 512 * NQB], BF16)
    dbg_t("pvd", [128, 4 * 1024], F32)

    ctxG = ExitStack()
    ctxA1 = ExitStack()
    ctxA2 = ExitStack()
    ctxB = ExitStack()
    ctxBw = ExitStack()
    ctxD = ExitStack()
    ctxC = ExitStack()
    with ExitStack() as ctx:
        def mk_sb(stack):
            return lambda n, sh, dt, side="left": stack.enter_context(
                nc.sbuf_tensor(n, sh, dt, side=side))
        sbG = mk_sb(ctxG)
        sbA1 = mk_sb(ctxA1)
        sbA2 = mk_sb(ctxA2)
        sbB = mk_sb(ctxB)
        sbBw = mk_sb(ctxBw)
        sbD = mk_sb(ctxD)
        sbC = mk_sb(ctxC)
        pst = lambda n, sh, dt: ctx.enter_context(nc.psum_tensor(n, sh, dt))
        sem = lambda n: ctx.enter_context(nc.semaphore(n))
        ctx.callback(ctxG.close)

        xqT = sbG("xqT", [128, KCH * T], BF16)
        iden = sbG("iden", [128, 128], BF16)
        idenf = sbG("idenf", [128, 128], F32)
        ones_b = sbG("ones_b", [128, 1], BF16)
        ones_f = sbG("ones_f", [128, 1], F32)
        lam = sbG("lam", [1, 8], F32)
        lam40 = sbG("lam40", [128, 1], F32)
        gp = sbG("gp", [128, 1], F32)
        cm20 = sbG("cm20", [128, 1], F32)
        bias16 = sbG("bias16", [128, 16], F32)
        vdum = sbG("vdum", [128, 4], F32)
        adum = sbG("adum", [128, 4], F32)
        lnS = sbG("lnS", [128, 14], F32)

        def mkws(p, wide, alloc):
            return {
                "bm": alloc(p + "bm", [128, wide // 16], F32),
                "scale": alloc(p + "scale", [128, wide // 16], F32),
                "rs": alloc(p + "rs", [128, wide // 16], F32),
                "a": alloc(p + "a", [128, wide], F32),
                "lo": alloc(p + "lo", [128, wide], F32),
                "hi": alloc(p + "hi", [128, wide], F32),
                "sgn": alloc(p + "sgn", [128, wide], F32),
                "tmp": alloc(p + "tmp", [128, wide], F32),
            }
        wsx = mkws("wsx_", E, sbA1)
        wsw = mkws("wsw_", WW, sbA1)
        xin = sbA1("xin", [128, 2 * E], F32)
        xnf = sbA1("xnf", [128, 2 * E], F32)
        xnb = sbA1("xnb", [128, 2 * E], BF16)
        xqb = sbA1("xqb", [128, 2 * E], BF16)
        xsqd = sbA1("xsqd", [128, E], F32)
        wqin = sbA1("wqin", [128, 2 * WW], F32)
        wqout = sbA1("wqout", [128, 2 * WW], BF16)
        iotaD = sbA1("iotaD", [128, 512], F32)
        lvec = sbA1("lvec", [1, 4 * D], F32)
        lprod = sbA1("lprod", [1, 2 * D], F32)

        xnT = sbA2("xnT", [128, KCH * T], BF16, side="right")

        PB = [pst(f"pb{i}", [128, 512], F32) for i in range(8)]

        s_xin = sem("s_xin")
        s_ln = sem("s_ln")
        s_act = sem("s_act")
        s_tr = sem("s_tr")
        s_ev = sem("s_ev")
        s_q = sem("s_q")
        s_wqi = sem("s_wqi")
        s_wqd = sem("s_wqd")
        s_wqo = sem("s_wqo")
        s_w2 = sem("s_w2")
        s_w2c = sem("s_w2c")
        s_mm2 = sem("s_mm2")
        s_ev2 = sem("s_ev2")
        s_cb = sem("s_cb")
        s_cbt = sem("s_cbt")
        s_cbd = sem("s_cbd")
        s_cbq = sem("s_cbq")
        s_cbm = sem("s_cbm")
        s_cbe = sem("s_cbe")
        s_sc = sem("s_sc")
        s_mk = sem("s_mk")
        s_exp = sem("s_exp")
        s_pv = sem("s_pv")
        s_pev = sem("s_pev")
        s_epv = sem("s_epv")
        s_epa = sem("s_epa")
        s_bc = sem("s_bc")
        s_aT = sem("s_aT")
        s_rms = sem("s_rms")
        s_wo = sem("s_wo")
        s_t4 = sem("s_t4")
        s_ts = sem("s_ts")
        s_xrl = sem("s_xrl")
        s_p0 = sem("s_p0")
        s_wdT = sem("s_wdT")
        s_wgT = sem("s_wgT")
        s_gu = sem("s_gu")
        s_si = sem("s_si")
        s_h = sem("s_h")
        s_hq = sem("s_hq")
        s_htr = sem("s_htr")
        s_hev = sem("s_hev")
        s_dn = sem("s_dn")
        s_trl = sem("s_trl")
        s_y = sem("s_y")
        s_out = sem("s_out")
        s_dbg = sem("s_dbg")
        s_af = sem("s_af")
        af_n = [0]

        def afence(eng, r):
            af_n[0] += 1
            r.then_inc(s_af, 1)
            eng.wait_ge(s_af, af_n[0])

        blk = ctx.enter_context(nc.Block())

        def dstage(hi, k):
            return hi * 5 + k

        def astage(hi, k):
            return hi * 4 + k

        @blk.sync
        def _(eng):
            eng.dma_start(lvec[0:1, 0:D], lq1_d[None, :]).then_inc(s_p0, 16)
            eng.dma_start(lvec[0:1, D:2 * D], lk1_d[None, :]).then_inc(s_p0, 16)
            eng.dma_start(lvec[0:1, 2 * D:3 * D], lq2_d[None, :]).then_inc(s_p0, 16)
            eng.dma_start(lvec[0:1, 3 * D:4 * D], lk2_d[None, :]).then_inc(s_p0, 16)
            eng.dma_start(gp[:, 0:1], g_d[:, None]).then_inc(s_p0, 16)

        @blk.gpsimd
        def _(eng):
            eng.iota(iotaD[:], [[1, 512]], channel_multiplier=-1,
                     allow_small_or_imprecise_dtypes=True).then_inc(s_p0, 1)

        @blk.vector
        def _(eng):
            eng.wait_ge(s_p0, 81)
            eng.memset(ones_b[:], 1.0)
            eng.memset(ones_f[:], 1.0)
            eng.memset(vdum[:], 1.0)
            eng.memset(cm20[:], -20.0 * math.log(2.0))
            eng.memset(adum[:], 1.0)
            eng.tensor_scalar(iden[:], iotaD[:, 0:128], 0.0, None, ALU.is_equal)
            eng.tensor_scalar(idenf[:, :], iotaD[:, 0:128], 0.0, None,
                              ALU.is_equal)
            eng.tensor_scalar(gp[:, 0:1], gp[:, 0:1], 1.0 - LAMBDA_INIT,
                              None, ALU.mult)
            eng.tensor_tensor(lprod[0:1, 0:D], lvec[0:1, 0:D],
                              lvec[0:1, D:2 * D], ALU.mult)
            eng.tensor_tensor(lprod[0:1, D:2 * D], lvec[0:1, 2 * D:3 * D],
                              lvec[0:1, 3 * D:4 * D], ALU.mult)
            eng.reduce_sum(lam[0:1, 0:2],
                           lprod[0:1, :].rearrange("p (b k) -> p b k", k=D), AX)
            eng.tensor_scalar(vdum[:, 0:2], vdum[:, 0:2], 1.0, None, ALU.mult)
            eng.tensor_scalar(lam[0:1, 3:5], lam[0:1, 0:2], 1.0,
                              None, ALU.mult).then_inc(s_ln, 1)

        @blk.scalar
        def _(eng):
            eng.wait_ge(s_ln, 1)
            eng.activation(lam[0:1, 5:7], lam[0:1, 3:5],
                           AF.Exp).then_inc(s_act, 1)

        @blk.vector
        def _(eng):
            eng.wait_ge(s_act, 1)
            eng.tensor_tensor(lam[0:1, 2:3], lam[0:1, 5:6], lam[0:1, 6:7],
                              ALU.subtract)
            eng.tensor_scalar(vdum[:, 0:2], vdum[:, 0:2], 1.0, None, ALU.mult)
            eng.tensor_scalar(lam[0:1, 2:3], lam[0:1, 2:3], LAMBDA_INIT,
                              None, ALU.add)
            eng.tensor_scalar(vdum[:, 0:2], vdum[:, 0:2], 1.0, None, ALU.mult)
            eng.tensor_scalar(lam[0:1, 7:8], lam[0:1, 2:3], P40,
                              None, ALU.mult).then_inc(s_ln, 1)

        @blk.gpsimd
        def _(eng):
            eng.wait_ge(s_ln, 2)
            eng.dma_start(cc_s[16:17], lam[0:1, 7:8]).then_inc(s_p0, 16)
            eng.wait_ge(s_p0, 97)
            eng.dma_start(lam40[:, 0:1],
                          cc_s[16:17, None].broadcast_to([128, 1])
                          ).then_inc(s_p0, 16)

        def wt_src(i):
            if i < 22:
                return wg_d[i * 128:(i + 1) * 128, :], E, \
                    wgq_s[i * 128:(i + 1) * 128, :]
            if i < 44:
                j = i - 22
                return wu_d[j * 128:(j + 1) * 128, :], E, \
                    wuq_s[j * 128:(j + 1) * 128, :]
            j, half = divmod(i - 44, 2)
            c0 = half * WW
            return wd_d[j * 128:(j + 1) * 128, c0:c0 + WW], WW, \
                wdq_s[j * 128:(j + 1) * 128, c0:c0 + WW]

        @blk.sync
        def _(eng):
            for t in range(NT):
                if t >= 2:
                    eng.wait_ge(s_q, t - 1)
                    eng.wait_ge(s_act, 1 + 3 * (t - 1))
                eng.dma_start(xin[:, (t % 2) * E:(t % 2 + 1) * E],
                              x_d[t * 128:(t + 1) * 128, :]).then_inc(s_xin, 16)

        @blk.gpsimd
        def _(eng):
            for i in range(NWCHUNK):
                src, F, dst = wt_src(i)
                if i >= 2:
                    eng.wait_ge(s_wqd, i - 1)
                eng.dma_start(wqin[:, (i % 2) * WW:(i % 2) * WW + F],
                              src).then_inc(s_wqi, 16)
                if i >= 1:
                    eng.wait_ge(s_wqd, i)
                    psrc, pF, pdst = wt_src(i - 1)
                    eng.dma_start(pdst, wqout[:, ((i - 1) % 2) * WW:
                                               ((i - 1) % 2) * WW + pF]
                                  ).then_inc(s_wqo, 16)
            eng.wait_ge(s_wqd, NWCHUNK)
            psrc, pF, pdst = wt_src(NWCHUNK - 1)
            eng.dma_start(pdst, wqout[:, ((NWCHUNK - 1) % 2) * WW:
                                       ((NWCHUNK - 1) % 2) * WW + pF]
                          ).then_inc(s_wqo, 16)

        @blk.scalar
        def _(eng):
            for t in range(NT):
                p = t % 2
                eng.wait_ge(s_xin, 16 * (t + 1))
                eng.activation(xsqd[:, :], xin[:, p * E:(p + 1) * E], AF.Square,
                               accum_out=lnS[:, p * 7 + 1:p * 7 + 2]
                               ).then_inc(s_act, 1)
                eng.wait_ge(s_ln, 2 + 3 * t + 1)
                r = eng.activation(lnS[:, p * 7 + 5:p * 7 + 6],
                                   lnS[:, p * 7 + 4:p * 7 + 5], AF.Ln)
                afence(eng, r)
                eng.activation(lnS[:, p * 7 + 5:p * 7 + 6],
                               lnS[:, p * 7 + 5:p * 7 + 6], AF.Exp,
                               scale=-0.5).then_inc(s_act, 1)
                eng.wait_ge(s_ln, 2 + 3 * t + 2)
                if t >= 2:
                    eng.wait_ge(s_q, t - 1)
                eng.activation(xnf[:, p * E:(p + 1) * E],
                               xin[:, p * E:(p + 1) * E], AF.Identity,
                               bias=lnS[:, p * 7 + 6:p * 7 + 7],
                               scale=lnS[:, p * 7 + 5:p * 7 + 6]
                               ).then_inc(s_act, 1)

        @blk.vector
        def _(eng):
            il = VecEmitter(eng, vdum)
            wtile = [0]

            def weight_stream(nmax):
                ths = []
                while wtile[0] < min(nmax, NWCHUNK):
                    i = wtile[0]
                    src, F, dst = wt_src(i)
                    pp = (i % 2) * WW
                    ths.append((lambda i=i: eng.wait_ge(s_wqi, 16 * (i + 1)),
                                "w"))
                    if i >= 2:
                        ths.append((lambda i=i: eng.wait_ge(s_wqo,
                                                            16 * (i - 1)),
                                    "w"))
                    qt = quant_thunks(eng, wqin[:, pp:pp + F],
                                      wqout[:, pp:pp + F], wsw, F)
                    last_t, last_f = qt[-1]

                    def fin(last_t=last_t):
                        r = last_t()
                        r.then_inc(s_wqd, 1)
                        return r
                    qt[-1] = (fin, last_f)
                    ths.extend(qt)
                    wtile[0] += 1
                return ths

            for t in range(NT):
                p = t % 2
                xth = []
                xth.append((lambda t=t: eng.wait_ge(s_xin, 16 * (t + 1)),
                            "w"))
                xth.append((lambda p=p: eng.reduce_sum(
                    lnS[:, p * 7 + 0:p * 7 + 1],
                    xin[:, p * E:(p + 1) * E], AX), False))
                xth.append((lambda p=p: eng.tensor_scalar(
                    lnS[:, p * 7 + 2:p * 7 + 3], lnS[:, p * 7 + 0:p * 7 + 1],
                    1.0 / 1024.0, None, ALU.mult), True))
                xth.append((lambda p=p: eng.tensor_tensor(
                    lnS[:, p * 7 + 3:p * 7 + 4], lnS[:, p * 7 + 2:p * 7 + 3],
                    lnS[:, p * 7 + 2:p * 7 + 3], ALU.mult), True))
                xth.append((lambda t=t: eng.wait_ge(s_act, 1 + 3 * t + 1),
                            "w"))
                xth.append((lambda p=p: eng.scalar_tensor_tensor(
                    lnS[:, p * 7 + 4:p * 7 + 5], lnS[:, p * 7 + 1:p * 7 + 2],
                    1.0 / 1024.0, lnS[:, p * 7 + 3:p * 7 + 4],
                    ALU.mult, ALU.subtract), True))

                def vareps(p=p):
                    r = eng.tensor_scalar(lnS[:, p * 7 + 4:p * 7 + 5],
                                          lnS[:, p * 7 + 4:p * 7 + 5],
                                          1e-5, None, ALU.add)
                    r.then_inc(s_ln, 1)
                    return r
                xth.append((vareps, True))
                xth.append((lambda t=t: eng.wait_ge(s_act, 1 + 3 * t + 2),
                            "w"))

                def negm(p=p):
                    r = eng.scalar_tensor_tensor(
                        lnS[:, p * 7 + 6:p * 7 + 7],
                        lnS[:, p * 7 + 2:p * 7 + 3], -1.0,
                        lnS[:, p * 7 + 5:p * 7 + 6], ALU.mult, ALU.mult)
                    r.then_inc(s_ln, 1)
                    return r
                xth.append((negm, True))
                xth.append((lambda t=t: eng.wait_ge(s_act, 1 + 3 * t + 3),
                            "w"))
                if t >= 2:
                    xth.append((lambda t=t: eng.wait_ge(s_tr, 2 * (t - 2) + 1),
                                "w"))

                def xnbcp(p=p):
                    r = eng.tensor_copy(xnb[:, p * E:(p + 1) * E],
                                        xnf[:, p * E:(p + 1) * E])
                    r.then_inc(s_ln, 1)
                    return r
                xth.append((xnbcp, True))
                if t >= 2:
                    xth.append((lambda t=t: eng.wait_ge(s_tr, 2 * (t - 2) + 2),
                                "w"))
                qt = quant_thunks(eng, xnf[:, p * E:(p + 1) * E],
                                  xqb[:, p * E:(p + 1) * E], wsx, E)
                last_t, last_f = qt[-1]

                def qfin(last_t=last_t):
                    r = last_t()
                    r.then_inc(s_q, 1)
                    return r
                qt[-1] = (qfin, last_f)
                xth.extend(qt)

                def evac_xn(t=t):
                    eng.wait_ge(s_tr, 2 * t + 1)
                    v = PB[t % 2][:].bitcast(BF16)
                    for j in range(KCH):
                        r = eng.tensor_copy(
                            xnT[:, j * T + t * 128:j * T + (t + 1) * 128],
                            v[:, j * 128:(j + 1) * 128])
                    r.then_inc(s_ev, 1)
                    return r
                xth.append((evac_xn, False))

                def evac_xq(t=t):
                    eng.wait_ge(s_tr, 2 * t + 2)
                    v = PB[2 + t % 2][:].bitcast(BF16)
                    for j in range(KCH):
                        r = eng.tensor_copy(
                            xqT[:, j * T + t * 128:j * T + (t + 1) * 128],
                            v[:, j * 128:(j + 1) * 128])
                    r.then_inc(s_ev, 1)
                    return r
                xth.append((evac_xq, False))

                wth = weight_stream(4 * (t + 1) if t < NT - 1 else NWCHUNK)
                il.run([("x", xth), ("w", wth)])

        @blk.tensor
        def _(eng):
            for t in range(NT):
                p = t % 2
                eng.wait_ge(s_ln, 2 + 3 * t + 3)
                if t >= 2:
                    eng.wait_ge(s_ev, 2 * (t - 2) + 1)
                v = PB[p][:].bitcast(BF16)
                for j in range(KCH):
                    r = eng.transpose(v[:, j * 128:(j + 1) * 128],
                                      xnb[:, p * E + j * 128:
                                          p * E + (j + 1) * 128], iden[:])
                r.then_inc(s_tr, 1)
                eng.wait_ge(s_q, t + 1)
                if t >= 2:
                    eng.wait_ge(s_ev, 2 * (t - 2) + 2)
                v = PB[2 + p][:].bitcast(BF16)
                for j in range(KCH):
                    r = eng.transpose(v[:, j * 128:(j + 1) * 128],
                                      xqb[:, p * E + j * 128:
                                          p * E + (j + 1) * 128], iden[:])
                r.then_inc(s_tr, 1)


        def _finisher(waits, dumps):
            @blk.gpsimd
            def _(eng):
                for s, v in waits:
                    eng.wait_ge(s, v)
                nd = 0
                for name, ap in dumps:
                    if name in dbg:
                        eng.dma_start(dbg[name][:, :], ap).then_inc(s_dbg, 16)
                        nd += 1
                for t in range(NT):
                    eng.dma_start(y_d[t * 128:(t + 1) * 128, :],
                                  wqin[:, 0:E] if max_phase == 1 else
                                  wpf[:, 0:E]).then_inc(s_out, 16)
                eng.wait_ge(s_out, 16 * NT)
                nd += ndbg[0]
                if nd:
                    eng.wait_ge(s_dbg, 16 * nd)

        if max_phase <= 1:
            _finisher([(s_q, NT), (s_ev, 2 * NT), (s_tr, 2 * NT),
                       (s_act, 1 + 3 * NT), (s_wqd, NWCHUNK),
                       (s_wqo, 16 * NWCHUNK), (s_ln, 2 + 3 * NT),
                       (s_p0, 113)],
                      [("xnT", xnT[:]), ("xqT", xqT[:]), ("lam", lam[:]),
                       ("wgq", wgq_s[:, :]), ("wuq", wuq_s[:, :]),
                       ("wdq", wdq_s[:, :])])
            ctxA1.close()
            ctxA2.close()
            return nc, dbg

        ctxA1.close()
        QT = sbB("QT", [128, H * T], BF16)
        KT = sbB("KT", [128, H * T], BF16)
        V = sbB("V", [128, NT * E], BF16)
        wprojb = sbB("wprojb", [128, KCH * E], BF16)
        wob = wprojb
        wpf = sbBw("wpf", [128, 2 * E], F32)
        cbsq = sbBw("cbsq", [128, 2 * 512], BF16)
        cbev = sbBw("cbev", [128, 2 * 512], F32)
        qn2max = sbBw("qn2max", [128, 32], F32)
        qn2t = sbBw("qn2t", [32, 128], F32)
        cbs = sbBw("cbs", [32, 4], F32)

        if "xnT" in dbg:
            @blk.gpsimd
            def _(eng):
                eng.wait_ge(s_ev, 2 * NT)
                eng.dma_start(dbg["xnT"][:, :], xnT[:]).then_inc(s_dbg, 16)
                ndbg[0] += 1

        @blk.sync
        def _(eng):
            eng.wait_ge(s_q, NT)
            eng.wait_ge(s_wqd, NWCHUNK)
            eng.wait_ge(s_wqo, 16 * NWCHUNK)
            eng.wait_ge(s_ev, 2 * NT)
            eng.wait_ge(s_tr, 2 * NT)
            eng.wait_ge(s_act, 1 + 3 * NT)
            for pi in range(4):
                src = [wq_d, wk_d, wv_d, wo_d][pi]
                for k in range(KCH):
                    idx = pi * KCH + k
                    if idx >= 2:
                        eng.wait_ge(s_w2c, idx - 1)
                    eng.dma_start(wpf[:, (idx % 2) * E:(idx % 2 + 1) * E],
                                  src[k * 128:(k + 1) * 128, :]
                                  ).then_inc(s_w2, 16)

        @blk.vector
        def _(eng):
            for pi in range(4):
                for k in range(KCH):
                    idx = pi * KCH + k
                    eng.wait_ge(s_w2, 16 * (idx + 1))
                    if pi in (1, 2) and k == 0:
                        eng.wait_ge(s_mm2, 32 * pi)
                    if pi == 3 and k == 0:
                        eng.wait_ge(s_mm2, 96)
                    dst = wprojb
                    r = eng.tensor_copy(dst[:, k * E:(k + 1) * E],
                                        wpf[:, (idx % 2) * E:(idx % 2 + 1) * E])
                    r.then_inc(s_w2c, 1)

        @blk.tensor
        def _(eng):
            n = 0
            for pi in range(2):
                eng.wait_ge(s_w2c, (pi + 1) * KCH)
                eng.wait_ge(s_ev, 2 * NT)
                for d in range(KCH):
                    for tb in range(4):
                        bank = PB[n % 4]
                        if n >= 4:
                            eng.wait_ge(s_ev2, n - 4 + 1)
                        for k in range(KCH):
                            r = eng.matmul(
                                bank[:],
                                wprojb[:, k * E + d * 128:k * E + (d + 1) * 128],
                                xnT[:, k * T + tb * 512:k * T + (tb + 1) * 512],
                                start=(k == 0), stop=(k == KCH - 1))
                        r.then_inc(s_mm2, 1)
                        n += 1
            eng.wait_ge(s_w2c, 3 * KCH)
            for t in range(NT):
                for half in range(2):
                    bank = PB[4 + n % 4]
                    if n >= 68:
                        eng.wait_ge(s_ev2, n - 4 + 1)
                    for k in range(KCH):
                        r = eng.matmul(
                            bank[:],
                            xnT[:, k * T + t * 128:k * T + (t + 1) * 128],
                            wprojb[:, k * E + half * 512:k * E + (half + 1) * 512],
                            start=(k == 0), stop=(k == KCH - 1))
                    r.then_inc(s_mm2, 1)
                    n += 1

        @blk.scalar
        def _(eng):
            n = 0
            for pi in range(2):
                dstT = [QT, KT][pi]
                for d in range(KCH):
                    for tb in range(4):
                        eng.wait_ge(s_mm2, n + 1)
                        r = eng.copy(dstT[:, d * T + tb * 512:
                                          d * T + (tb + 1) * 512],
                                     PB[n % 4][:])
                        r.then_inc(s_ev2, 1)
                        n += 1
            for t in range(NT):
                for half in range(2):
                    eng.wait_ge(s_mm2, n + 1)
                    r = eng.copy(V[:, t * E + half * 512:
                                   t * E + (half + 1) * 512],
                                 PB[4 + n % 4][:])
                    r.then_inc(s_ev2, 1)
                    n += 1

        @blk.scalar
        def _(eng):
            idx = 0
            for pi in range(2):
                srcT = [QT, KT][pi]
                eng.wait_ge(s_ev2, 32 * (pi + 1))
                for hh in range(H):
                    ph = pi * 8 + hh
                    for tb in range(4):
                        if idx >= 2:
                            eng.wait_ge(s_cbm, idx - 1)
                        r = eng.activation(
                            cbsq[:, (tb % 2) * 512:(tb % 2 + 1) * 512],
                            srcT[:, hh * T + tb * 512:hh * T + (tb + 1) * 512],
                            AF.Square)
                        r.then_inc(s_cbq, 1)
                        idx += 1
                    eng.wait_ge(s_cbm, 4 * (ph + 1))
                    eng.copy(cbev[:, 0:512], PB[6][:])
                    r = eng.copy(cbev[:, 512:1024], PB[7][:])
                    r.then_inc(s_cbe, 1)

        @blk.tensor
        def _(eng):
            idx = 0
            for pi in range(2):
                for hh in range(H):
                    ph = pi * 8 + hh
                    if ph >= 1:
                        eng.wait_ge(s_cb, ph)
                    for tb in range(4):
                        eng.wait_ge(s_cbq, idx + 1)
                        for br in range(2):
                            bank = PB[6 + br]
                            row = 32 * tb
                            r = eng.matmul(
                                bank[row:row + 1, :],
                                ones_b[br * 64:(br + 1) * 64, :],
                                cbsq[br * 64:(br + 1) * 64,
                                     (tb % 2) * 512:(tb % 2 + 1) * 512],
                                start=True, stop=True,
                                tile_position=(64 * br, row))
                        r.then_inc(s_cbm, 1)
                        idx += 1
            eng.wait_ge(s_cb, 16)
            eng.wait_ge(s_ev2, 96)
            r = eng.transpose(PB[0][0:32, 0:128], qn2max[:, 0:32], idenf[:])
            r.then_inc(s_cbt, 1)

        @blk.vector
        def _(eng):
            for pi in range(2):
                for hh in range(H):
                    ph = pi * 8 + hh
                    eng.wait_ge(s_cbe, ph + 1)
                    for br in range(2):
                        col = pi * 16 + 2 * hh + br
                        r = eng.reduce_max(
                            qn2max[:, col:col + 1],
                            cbev[:, br * 512:(br + 1) * 512], AX)
                    r.then_inc(s_cb, 1)

        @blk.scalar
        def _(eng):
            eng.wait_ge(s_cbt, 1)
            eng.copy(qn2t[:, :], PB[0][0:32, 0:128]).then_inc(s_cbd, 1)

        @blk.vector
        def _(eng):
            eng.wait_ge(s_cbd, 1)
            r = eng.reduce_max(
                cbs[0:32, 0:1],
                bass.AP(qn2t, 0, [[128, 32], [32, 4]]), AX)
            r.then_inc(s_cb, 1)

        @blk.gpsimd
        def _(eng):
            eng.wait_ge(s_cb, 17)
            eng.dma_start(cbs[0:16, 1:2], cbs[16:32, 0:1]).then_inc(s_bc, 16)

        @blk.vector
        def _(eng):
            eng.wait_ge(s_bc, 16)
            r = eng.tensor_tensor(cbs[0:16, 2:3], cbs[0:16, 0:1],
                                  cbs[0:16, 1:2], ALU.mult)
            r.then_inc(s_cb, 1)

        @blk.scalar
        def _(eng):
            eng.wait_ge(s_cb, 18)
            r = eng.activation(cbs[0:16, 3:4], cbs[0:16, 2:3], AF.Sqrt,
                               scale=(1.01 / 8.0) ** 2)
            r.then_inc(s_cbd, 1)

        @blk.vector
        def _(eng):
            eng.wait_ge(s_cbd, 2)
            r = eng.tensor_scalar(cbs[0:16, 3:4], cbs[0:16, 3:4],
                                  -1.0, -1.0, ALU.mult, ALU.add)
            r.then_inc(s_cb, 1)

        @blk.gpsimd
        def _(eng):
            eng.wait_ge(s_cb, 19)
            eng.dma_start(cc_s[0:16, None], cbs[0:16, 3:4]).then_inc(s_bc, 16)
            eng.wait_ge(s_bc, 32)
            eng.dma_start(bias16[:, :],
                          cc_s[None, 0:16].broadcast_to([128, 16])
                          ).then_inc(s_bc, 16)

        ctxBw.close()
        ctxA2.close()
        ptile = sbD("ptile", [128, 4 * 512], BF16)
        pvsb = sbD("pvsb", [128, 4 * 512], F32)
        apr = sbD("apr", [128, 2 * 512], F32)
        sqa = sbD("sqa", [128, 2 * 512], F32)
        aT = sbD("aT", [128, H * 512], BF16)
        srow = sbD("srow", [128, 5 * 512], F32)
        fbr = sbD("fbr", [128, 2 * 512], F32)
        rstdb = fbr
        xrl = sbD("xrl", [128, E], F32)
        masks = sbD("masks", [128, 4 * 512], F32)
        arch = sbD("arch", [128, 2 * 512], F32) if "arch" in debug else None
        wo_t = sbD("wo_t", [128, E], F32)

        if any(k in dbg for k in ("QT", "KT", "V", "bias16")):
            @blk.gpsimd
            def _(eng):
                eng.wait_ge(s_ev2, 96)
                eng.wait_ge(s_bc, 48)
                for name, tile in (("QT", QT), ("KT", KT), ("V", V),
                                   ("bias16", bias16)):
                    if name in dbg:
                        eng.dma_start(dbg[name][:, :],
                                      tile[:]).then_inc(s_dbg, 16)
                        ndbg[0] += 1

        @blk.tensor
        def _(eng):
            for qb in range(p3_qb):
                ns = ns_of(qb)
                for h in range(p3_h):
                    hi = qb * H + h
                    hpar = hi % 2
                    scb = sc_base(qb, h)
                    pvb = pv_base(qb, h)
                    if hi >= 1 and h >= 1 and p3_stage >= 5:
                        pj = hi - 1
                        ppar = pj % 2
                        eng.wait_ge(s_epa, astage(pj, 2))
                        if pj >= 2:
                            eng.wait_ge(s_epa, astage(pj - 2, 3))
                        rrow = 64 * ppar
                        r = eng.matmul(PB[7][rrow:rrow + 1, :], ones_f[:],
                                       sqa[:, ppar * 512:(ppar + 1) * 512],
                                       start=True, stop=True,
                                       tile_position=(0, rrow))
                        r.then_inc(s_rms, 1)
                    def emit_scores(st):
                        spar = st % 2
                        if hi == 0 and st == 0:
                            eng.wait_ge(s_cbd, 1)
                        if st >= 2:
                            eng.wait_ge(s_exp, scb + 2 * (st - 2) + 2)
                        elif hi > 0:
                            eng.wait_ge(s_exp, scb)
                        if qb > 0 and h == 0 and st < 2:
                            eng.wait_ge(s_t4, qb * 8)
                        r = eng.matmul(PB[2 * spar][:],
                                       KT[0:64, h * T + st * 128:
                                          h * T + (st + 1) * 128],
                                       QT[0:64, h * T + qb * 512:
                                          h * T + (qb + 1) * 512],
                                       start=True, stop=True)
                        r.then_inc(s_sc, 1)
                        r = eng.matmul(PB[2 * spar + 1][:],
                                       KT[64:128, h * T + st * 128:
                                          h * T + (st + 1) * 128],
                                       QT[64:128, h * T + qb * 512:
                                          h * T + (qb + 1) * 512],
                                       start=True, stop=True,
                                       tile_position=(64, 0))
                        r.then_inc(s_sc, 1)

                    def emit_pv(st):
                        spar = st % 2
                        eng.wait_ge(s_exp, scb + 2 * st + 2)
                        if st == 0 and hi >= 1:
                            eng.wait_ge(s_pev, 2 * hi)
                        if st == 0 and hi >= 2:
                            eng.wait_ge(s_epv, dstage(hi - 2, 1))
                        vsl = V[:, st * E + h * 128:st * E + (h + 1) * 128]
                        eng.matmul(PB[4][:], vsl,
                                   ptile[:, (2 * spar) * 512:
                                         (2 * spar + 1) * 512],
                                   start=(st == 0), stop=(st == ns - 1))
                        eng.matmul(PB[5][:], vsl,
                                   ptile[:, (2 * spar + 1) * 512:
                                         (2 * spar + 2) * 512],
                                   start=(st == 0), stop=(st == ns - 1))
                        row1 = 64 * hpar
                        row2 = 64 * hpar + 32
                        eng.matmul(PB[6][row1:row1 + 1, :], ones_b[:],
                                   ptile[:, (2 * spar) * 512:
                                         (2 * spar + 1) * 512],
                                   start=(st == 0), stop=(st == ns - 1),
                                   tile_position=(0, row1))
                        r = eng.matmul(PB[6][row2:row2 + 1, :], ones_b[:],
                                       ptile[:, (2 * spar + 1) * 512:
                                             (2 * spar + 2) * 512],
                                       start=(st == 0), stop=(st == ns - 1),
                                       tile_position=(0, row2))
                        r.then_inc(s_pv, 1)

                    for st in range(ns):
                        emit_scores(st)
                        if p3_stage >= 2 and st >= 2:
                            emit_pv(st - 2)
                    if p3_stage >= 2:
                        emit_pv(ns - 2)
                        emit_pv(ns - 1)
                pj = qb * H + min(p3_h, H) - 1
                ppar = pj % 2
                if p3_stage >= 5:
                    eng.wait_ge(s_epa, astage(pj, 2))
                    if pj >= 2:
                        eng.wait_ge(s_epa, astage(pj - 2, 3))
                    rrow = 64 * ppar
                    r = eng.matmul(PB[7][rrow:rrow + 1, :], ones_f[:],
                                   sqa[:, ppar * 512:(ppar + 1) * 512],
                                   start=True, stop=True,
                                   tile_position=(0, rrow))
                    r.then_inc(s_rms, 1)
                if p3_h < H:
                    continue
                eng.wait_ge(s_aT, (qb + 1) * 8)
                for sub in range(4):
                    for half in range(2):
                        wn = qb * 8 + sub * 2 + half
                        if wn >= 4:
                            eng.wait_ge(s_t4, wn - 3)
                        for hh in range(H):
                            r = eng.matmul(
                                PB[wn % 4][:],
                                aT[:, hh * 512 + sub * 128:
                                   hh * 512 + (sub + 1) * 128],
                                wob[:, hh * E + half * 512:
                                    hh * E + (half + 1) * 512],
                                start=(hh == 0), stop=(hh == H - 1))
                        r.then_inc(s_wo, 1)

        @blk.vector
        def _(eng):
            eng.wait_ge(s_p0, 114)
            for j in range(4):
                eng.tensor_scalar(masks[:, j * 512:(j + 1) * 512],
                                  srow[:, 0:512],
                                  float(128 * j), NEG_BIG, ALU.is_lt, ALU.mult)
            for qb in range(p3_qb):
                ns = ns_of(qb)
                for h in range(p3_h):
                    hi = qb * H + h
                    hpar = hi % 2
                    scb = sc_base(qb, h)
                    r1 = 64 * hpar
                    r2 = 64 * hpar + 32
                    for st in range(ns - 4, ns):
                        j = st - 4 * qb
                        spar = st % 2
                        eng.wait_ge(s_sc, scb + 2 * st + 2)
                        eng.tensor_tensor(PB[2 * spar][:], PB[2 * spar][:],
                                          masks[:, j * 512:(j + 1) * 512],
                                          ALU.add)
                        r = eng.tensor_tensor(PB[2 * spar + 1][:],
                                              PB[2 * spar + 1][:],
                                              masks[:, j * 512:(j + 1) * 512],
                                              ALU.add)
                        r.then_inc(s_mk, 2)
                    if p3_stage < 3:
                        continue
                    eng.wait_ge(s_pv, pv_base(qb, h) + ns)
                    eng.tensor_copy(srow[r1:r1 + 1, 0:512],
                                    PB[6][r1:r1 + 1, :])
                    r = eng.tensor_copy(srow[r2:r2 + 1, 0:512],
                                        PB[6][r2:r2 + 1, :])
                    r.then_inc(s_epv, 1)
                    eng.wait_ge(s_bc, bc_base(hi) + 16)
                    r = eng.tensor_scalar(vdum[:, 2:4], vdum[:, 2:4],
                                          1.0, None, ALU.mult)
                    r.then_inc(s_epv, 1)
                    if p3_stage < 4:
                        continue
                    if hi == 0:
                        eng.wait_ge(s_p0, 113)
                    eng.wait_ge(s_epa, astage(hi, 1))
                    if hi >= 2:
                        eng.wait_ge(s_bc, bc_base(hi - 2) + 32)
                    r = eng.scalar_tensor_tensor(
                        srow[r1:r1 + 1, 2 * 512:3 * 512],
                        srow[r1:r1 + 1, 0:512], lam40[r1:r1 + 1, 0:1],
                        srow[r1:r1 + 1, 512:1024], ALU.mult, ALU.mult)
                    r.then_inc(s_epv, 1)
                    eng.wait_ge(s_bc, bc_base(hi) + 48)
                    eng.wait_ge(s_pev, 2 * hi + 2)
                    eng.tensor_tensor(fbr[:, hpar * 512:(hpar + 1) * 512],
                                      pvsb[:, (2 * hpar + 1) * 512:
                                           (2 * hpar + 2) * 512],
                                      fbr[:, hpar * 512:(hpar + 1) * 512],
                                      ALU.mult)
                    eng.tensor_scalar(vdum[:, 0:2], vdum[:, 0:2], 1.0,
                                      None, ALU.mult)
                    r = eng.scalar_tensor_tensor(
                        apr[:, hpar * 512:(hpar + 1) * 512],
                        pvsb[:, 2 * hpar * 512:(2 * hpar + 1) * 512], P40,
                        fbr[:, hpar * 512:(hpar + 1) * 512],
                        ALU.mult, ALU.subtract)
                    r.then_inc(s_epv, 1)
                    if p3_stage < 5:
                        continue
                    eng.wait_ge(s_epa, astage(hi, 3))
                    if hi >= 2:
                        eng.wait_ge(s_bc, bc_base(hi - 2) + 64)
                    eng.tensor_scalar(srow[r1:r1 + 1, 3 * 512:4 * 512],
                                      srow[r1:r1 + 1, 0:512], P40, None,
                                      ALU.mult)
                    eng.tensor_scalar(srow[r1:r1 + 1, 512:1024],
                                      srow[r1:r1 + 1, 4 * 512:5 * 512],
                                      1.0 / 128.0, None, ALU.mult)
                    eng.tensor_tensor(srow[r1:r1 + 1, 3 * 512:4 * 512],
                                      srow[r1:r1 + 1, 3 * 512:4 * 512],
                                      srow[r1:r1 + 1, 3 * 512:4 * 512],
                                      ALU.mult)
                    eng.tensor_scalar(vdum[:, 0:2], vdum[:, 0:2], 1.0,
                                      None, ALU.mult)
                    r = eng.scalar_tensor_tensor(
                        srow[r1:r1 + 1, 3 * 512:4 * 512],
                        srow[r1:r1 + 1, 3 * 512:4 * 512], 1e-5,
                        srow[r1:r1 + 1, 512:1024], ALU.mult, ALU.add)
                    r.then_inc(s_epv, 1)
                    if "arch" in dbg and hi in (0, 1):
                        eng.tensor_scalar(vdum[:, 0:2], vdum[:, 0:2], 1.0,
                                          None, ALU.mult)
                        eng.tensor_copy(
                            arch[r1:r1 + 1, 512:1024],
                            srow[r1:r1 + 1, 3 * 512:4 * 512])
                    if p3_stage < 6:
                        continue
                    eng.wait_ge(s_bc, bc_base(hi) + 80)
                    if qb >= 1:
                        eng.wait_ge(s_wo, 8 * qb)
                    r = eng.scalar_tensor_tensor(
                        aT[:, h * 512:(h + 1) * 512],
                        apr[:, hpar * 512:(hpar + 1) * 512], gp[:, 0:1],
                        rstdb[:, hpar * 512:(hpar + 1) * 512],
                        ALU.mult, ALU.mult)
                    r.then_inc(s_aT, 1)
                    if "arch" in dbg and hi in (0, 1):
                        eng.tensor_copy(
                            arch[r1:r1 + 1, 0:512],
                            srow[r1:r1 + 1, 3 * 512:4 * 512])
                if p3_h < H:
                    continue
                for sub in range(4):
                    tt = qb * 4 + sub
                    for half in range(2):
                        wn = qb * 8 + sub * 2 + half
                        eng.wait_ge(s_wo, wn + 1)
                        eng.wait_ge(s_xrl, 16 * (tt + 1))
                        if tt >= 1 and half == 0:
                            eng.wait_ge(s_ts, 16 * tt)
                        r = eng.tensor_tensor(
                            wo_t[:, half * 512:(half + 1) * 512],
                            xrl[:, half * 512:(half + 1) * 512],
                            PB[wn % 4][:], ALU.add)
                        r.then_inc(s_t4, 1)

        @blk.scalar
        def _(eng):
            for qb in range(p3_qb):
                ns = ns_of(qb)
                for h in range(p3_h):
                    hi = qb * H + h
                    hpar = hi % 2
                    scb = sc_base(qb, h)
                    mkb = mk_base(qb, h)
                    pvb = pv_base(qb, h)
                    r1 = 64 * hpar
                    for st in range(ns):
                        spar = st % 2
                        diag = st >= ns - 4
                        if hi == 0 and st == 0:
                            eng.wait_ge(s_bc, 48)
                        if diag:
                            eng.wait_ge(s_mk, mkb + 2 * (st - (ns - 4)) + 2)
                        else:
                            eng.wait_ge(s_sc, scb + 2 * st + 2)
                        if st >= 2 and p3_stage >= 2:
                            eng.wait_ge(s_pv, pvb + st - 1)
                        for br in range(2):
                            r = eng.activation(
                                ptile[:, (2 * spar + br) * 512:
                                      (2 * spar + br + 1) * 512],
                                PB[2 * spar + br][:], AF.Exp,
                                bias=bias16[:, 2 * h + br:2 * h + br + 1],
                                scale=0.125)
                            r.then_inc(s_exp, 1)
                    if p3_stage < 3:
                        continue
                    eng.wait_ge(s_pv, pvb + ns)
                    if hi >= 2:
                        eng.wait_ge(s_epv, dstage(hi - 2, 4))
                    eng.copy(pvsb[:, 2 * hpar * 512:(2 * hpar + 1) * 512],
                             PB[4][:])
                    r = eng.copy(pvsb[:, (2 * hpar + 1) * 512:
                                      (2 * hpar + 2) * 512], PB[5][:])
                    r.then_inc(s_pev, 2)
                    if p3_stage < 4:
                        continue
                    eng.wait_ge(s_epv, dstage(hi, 2))
                    r = eng.activation(srow[r1:r1 + 1, 512:1024],
                                       srow[r1:r1 + 1, 512:1024], AF.Ln)
                    afence(eng, r)
                    r = eng.activation(srow[r1:r1 + 1, 512:1024],
                                       srow[r1:r1 + 1, 512:1024],
                                       AF.Exp, scale=-1.0)
                    r.then_inc(s_epa, 1)
                    if p3_stage < 5:
                        continue
                    eng.wait_ge(s_epv, dstage(hi, 4))
                    r = eng.activation(sqa[:, hpar * 512:(hpar + 1) * 512],
                                       apr[:, hpar * 512:(hpar + 1) * 512],
                                       AF.Square)
                    r.then_inc(s_epa, 1)
                    eng.wait_ge(s_rms, hi + 1)
                    if hi >= 2:
                        eng.wait_ge(s_epv, dstage(hi - 2, 5))
                    r = eng.copy(srow[r1:r1 + 1, 4 * 512:5 * 512],
                                 PB[7][r1:r1 + 1, :])
                    r.then_inc(s_epa, 1)
                    eng.wait_ge(s_epv, dstage(hi, 5))
                    r = eng.activation(srow[r1:r1 + 1, 3 * 512:4 * 512],
                                       srow[r1:r1 + 1, 3 * 512:4 * 512], AF.Ln,
                                       scale=float(2.0 ** -40))
                    afence(eng, r)
                    r = eng.activation(srow[r1:r1 + 1, 3 * 512:4 * 512],
                                       srow[r1:r1 + 1, 3 * 512:4 * 512],
                                       AF.Exp, scale=-0.5,
                                       bias=cm20[r1:r1 + 1, 0:1])
                    r.then_inc(s_epa, 1)

        @blk.gpsimd
        def _(eng):
            eng.iota(srow[:, 0:512], [[1, 512]], channel_multiplier=-1,
                     allow_small_or_imprecise_dtypes=True).then_inc(s_p0, 1)
            for qb in range(p3_qb):
                for h in range(p3_h):
                    hi = qb * H + h
                    hpar = hi % 2
                    r1 = 64 * hpar
                    r2 = 64 * hpar + 32
                    b = bc_base(hi)
                    if p3_stage < 3:
                        continue
                    eng.wait_ge(s_epv, dstage(hi, 1))
                    if hi >= 2:
                        eng.wait_ge(s_epv, dstage(hi - 2, 5))
                        eng.wait_ge(s_epa, astage(hi - 2, 4))
                    eng.dma_start(srow[r1:r1 + 1, 512:1024],
                                  srow[r2:r2 + 1, 0:512]).then_inc(s_bc, 16)
                    if p3_stage < 4:
                        continue
                    eng.wait_ge(s_epv, dstage(hi, 3))
                    if hi >= 2:
                        eng.wait_ge(s_bc, bc_base(hi - 2) + 48)
                    eng.dma_start(bc_s[hpar * 2:hpar * 2 + 1, :],
                                  srow[r1:r1 + 1, 2 * 512:3 * 512]
                                  ).then_inc(s_bc, 16)
                    eng.wait_ge(s_bc, b + 32)
                    if hi >= 2:
                        eng.wait_ge(s_aT, hi - 1)
                    eng.dma_start(fbr[:, hpar * 512:(hpar + 1) * 512],
                                  bc_s[hpar * 2:hpar * 2 + 1, :]
                                  .broadcast_to([128, 512])).then_inc(s_bc, 16)
                    if p3_stage < 6:
                        continue
                    eng.wait_ge(s_epa, astage(hi, 4))
                    if hi >= 2:
                        eng.wait_ge(s_bc, bc_base(hi - 2) + 80)
                    eng.dma_start(bc_s[hpar * 2 + 1:hpar * 2 + 2, :],
                                  srow[r1:r1 + 1, 3 * 512:4 * 512]
                                  ).then_inc(s_bc, 16)
                    eng.wait_ge(s_bc, b + 64)
                    if hi >= 2:
                        eng.wait_ge(s_aT, hi - 1)
                    eng.dma_start(rstdb[:, hpar * 512:(hpar + 1) * 512],
                                  bc_s[hpar * 2 + 1:hpar * 2 + 2, :]
                                  .broadcast_to([128, 512])).then_inc(s_bc, 16)
                    if "p3trace" in dbg:
                        eng.wait_ge(s_aT, hi + 1)
                        eng.dma_start(dbg["p3trace"][hi:hi + 1, :],
                                      srow[r1:r1 + 1, 0:5 * 512]
                                      ).then_inc(s_dbg, 16)
                        ndbg[0] += 1
                        if hi < 4:
                            eng.dma_start(
                                dbg["pvd"][:, hi * 1024:(hi + 1) * 1024],
                                pvsb[:, 2 * hpar * 512:(2 * hpar + 2) * 512]
                                ).then_inc(s_dbg, 16)
                            ndbg[0] += 1
                    if "aTd" in dbg and h == H - 1:
                        eng.wait_ge(s_aT, hi + 1)
                        eng.dma_start(dbg["aTd"][:, qb * 4096:(qb + 1) * 4096],
                                      aT[:]).then_inc(s_dbg, 16)
                        ndbg[0] += 1
                if p3_h < H:
                    continue
                for sub in range(4):
                    tt = qb * 4 + sub
                    if tt >= 1:
                        eng.wait_ge(s_t4, 2 * tt)
                    eng.dma_start(xrl[:, :],
                                  x_d[tt * 128:(tt + 1) * 128, :]
                                  ).then_inc(s_xrl, 16)
                    eng.wait_ge(s_t4, 2 * tt + 2)
                    eng.dma_start(ts_s[tt * 128:(tt + 1) * 128, :],
                                  wo_t[:, :]
                                  ).then_inc(s_ts, 16)

        if max_phase <= 3:
            if p3_qb >= NQB and p3_h >= H:
                fw = [(s_ts, 16 * NT), (s_aT, NHEADS)]
            elif p3_h >= H:
                fw = [(s_ts, 16 * 4 * p3_qb), (s_aT, 8 * p3_qb)]
            else:
                if p3_stage >= 6:
                    fw = [(s_aT, p3_h)]
                elif p3_stage >= 5:
                    fw = [(s_epv, 5 * (p3_h - 1) + 5), (s_epa, 4 * (p3_h - 1) + 4)]
                elif p3_stage >= 4:
                    fw = [(s_epv, 5 * (p3_h - 1) + 4)]
                elif p3_stage >= 3:
                    fw = [(s_epv, 5 * (p3_h - 1) + 2)]
                elif p3_stage >= 2:
                    fw = [(s_pv, pv_base(0, p3_h - 1) + 4)]
                else:
                    fw = [(s_exp, sc_base(0, p3_h - 1) + 8)]
            dl = [("tattn", ts_s[:, :]), ("lam", lam[:])]
            if arch is not None:
                dl.append(("arch", arch[:]))
            _finisher(fw, dl)
            ctxD.close()
            ctxB.close()
            return nc, dbg

        ctxD.close()
        ctxB.close()
        wdqT = sbC("wdqT", [128, ICH * E], BF16)
        wgT = sbC("wgT", [128, 2 * KCH * 512], BF16)
        wuT = sbC("wuT", [128, 2 * KCH * 512], BF16)
        sg = sbC("sg", [128, 2 * 512], F32)
        hblk = sbC("hblk", [128, 2 * 512], F32)
        hqb = sbC("hqb", [128, 2 * 512], BF16)
        hqT = sbC("hqT", [128, TOKGRP * ICH * 128], BF16)
        trl = sbC("trl", [128, 2 * E], F32)
        yt = sbC("yt", [128, 2 * E], F32)
        ws4 = mkws("ws4_", 512, sbC)

        @blk.sync
        def _(eng):
            eng.wait_ge(s_ts, 16 * NT)
            eng.wait_ge(s_wqo, 16 * NWCHUNK)
            for ic in range(ICH):
                eng.dma_start(wdqT[:, ic * E:(ic + 1) * E],
                              wdq_s[:, ic * 128:(ic + 1) * 128],
                              transpose=True).then_inc(s_wdT, 16)
            for G in range(NGRP):
                for ib in range(NIB):
                    bi = G * NIB + ib
                    p = bi % 2
                    sz = IBLK[ib]
                    i0 = IB_OFF[ib]
                    if bi >= 2:
                        eng.wait_ge(s_gu, 16 * (bi - 1))
                    for k in range(KCH):
                        eng.dma_start(
                            wgT[:, (p * KCH + k) * 512:(p * KCH + k) * 512 + sz],
                            wgq_s[i0:i0 + sz, k * 128:(k + 1) * 128],
                            transpose=True).then_inc(s_wgT, 16)
                        eng.dma_start(
                            wuT[:, (p * KCH + k) * 512:(p * KCH + k) * 512 + sz],
                            wuq_s[i0:i0 + sz, k * 128:(k + 1) * 128],
                            transpose=True).then_inc(s_wgT, 16)

        @blk.tensor
        def _(eng):
            pend = []
            gu_done = 0

            def flush_transposes(upto):
                while pend and pend[0][0] <= upto:
                    gi, tg, sz = pend.pop(0)
                    nbk = sz // 128
                    eng.wait_ge(s_hq, gi + 1)
                    if gi >= 2:
                        eng.wait_ge(s_hev, gi - 1)
                    v = PB[4 + tg % 2][:].bitcast(BF16)
                    for j in range(nbk):
                        r = eng.transpose(v[:, j * 128:(j + 1) * 128],
                                          hqb[:, (tg % 2) * 512 + j * 128:
                                              (tg % 2) * 512 + (j + 1) * 128],
                                          iden[:])
                    r.then_inc(s_htr, 1)

            first = True
            for G in range(NGRP):
                for ib in range(NIB):
                    bi = G * NIB + ib
                    p = bi % 2
                    sz = IBLK[ib]
                    for tg in range(TOKGRP):
                        t = G * TOKGRP + tg
                        gi = bi * TOKGRP + tg
                        flush_transposes(gi - 2)
                        eng.wait_ge(s_wgT, 16 * 16 * (bi + 1))
                        if first:
                            eng.wait_ge(s_t4, 32)
                            eng.wait_ge(s_pev, 64)
                            first = False
                        if gi >= 2:
                            eng.wait_ge(s_si, gi - 1)
                            eng.wait_ge(s_h, gi - 1)
                        for k in range(KCH):
                            r = eng.matmul(
                                PB[tg % 2][:, 0:sz],
                                xqT[:, k * T + t * 128:k * T + (t + 1) * 128],
                                wgT[:, (p * KCH + k) * 512:
                                    (p * KCH + k) * 512 + sz],
                                start=(k == 0), stop=(k == KCH - 1))
                        r.then_inc(s_gu, 1)
                        for k in range(KCH):
                            r = eng.matmul(
                                PB[2 + tg % 2][:, 0:sz],
                                xqT[:, k * T + t * 128:k * T + (t + 1) * 128],
                                wuT[:, (p * KCH + k) * 512:
                                    (p * KCH + k) * 512 + sz],
                                start=(k == 0), stop=(k == KCH - 1))
                        r.then_inc(s_gu, 1)
                        pend.append((gi, tg, sz))
                flush_transposes(10 ** 9)
                for tg in range(TOKGRP):
                    dn = G * TOKGRP + tg
                    eng.wait_ge(s_hev,
                                G * NIB * TOKGRP + (NIB - 1) * TOKGRP + tg + 1)
                    eng.wait_ge(s_wdT, 16 * ICH)
                    if dn == 0:
                        eng.wait_ge(s_epv, 5 * NHEADS)
                        eng.wait_ge(s_epa, 4 * NHEADS)
                    if dn >= 1:
                        eng.wait_ge(s_y, 2 * (dn - 1) + 2)
                    for half in range(2):
                        for ic in range(ICH):
                            r = eng.matmul(
                                PB[6 + half][:],
                                hqT[:, (tg * ICH + ic) * 128:
                                    (tg * ICH + ic + 1) * 128],
                                wdqT[:, ic * E + half * 512:
                                     ic * E + (half + 1) * 512],
                                start=(ic == 0), stop=(ic == ICH - 1))
                        r.then_inc(s_dn, 1)

        @blk.scalar
        def _(eng):
            for G in range(NGRP):
                for ib in range(NIB):
                    for tg in range(TOKGRP):
                        gi = (G * NIB + ib) * TOKGRP + tg
                        sz = IBLK[ib]
                        eng.wait_ge(s_gu, 2 * gi + 1)
                        if gi >= 2:
                            eng.wait_ge(s_h, gi - 1)
                        r = eng.activation(sg[:, (tg % 2) * 512:
                                               (tg % 2) * 512 + sz],
                                           PB[tg % 2][:, 0:sz], AF.Silu)
                        r.then_inc(s_si, 1)

        @blk.vector
        def _(eng):
            il = VecEmitter(eng, vdum)
            for G in range(NGRP):
                for ib in range(NIB):
                    sz = IBLK[ib]
                    i0 = IB_OFF[ib]
                    for tg in range(TOKGRP):
                        gi = (G * NIB + ib) * TOKGRP + tg
                        p = tg % 2
                        ths = []
                        ths.append((lambda gi=gi: eng.wait_ge(s_si, gi + 1),
                                    "w"))
                        ths.append((lambda gi=gi: eng.wait_ge(s_gu,
                                                              2 * gi + 2),
                                    "w"))

                        def hmul(p=p, sz=sz):
                            r = eng.tensor_tensor(
                                hblk[:, p * 512:p * 512 + sz],
                                sg[:, p * 512:p * 512 + sz],
                                PB[2 + p][:, 0:sz], ALU.mult)
                            r.then_inc(s_h, 1)
                            return r
                        ths.append((hmul, False))
                        if gi >= 2:
                            ths.append((lambda gi=gi: eng.wait_ge(
                                s_htr, gi - 1), "w"))
                        qt = quant_thunks(eng, hblk[:, p * 512:p * 512 + sz],
                                          hqb[:, p * 512:p * 512 + sz],
                                          ws4, sz)
                        last_t, last_f = qt[-1]

                        def qfin(last_t=last_t):
                            r = last_t()
                            r.then_inc(s_hq, 1)
                            return r
                        qt[-1] = (qfin, last_f)
                        ths.extend(qt)

                        def evac(gi=gi, tg=tg, i0=i0, sz=sz, p=p, G=G):
                            eng.wait_ge(s_htr, gi + 1)
                            if G >= 1:
                                eng.wait_ge(s_dn, G * TOKGRP * 2)
                            v = PB[4 + p][:].bitcast(BF16)
                            nbk = sz // 128
                            for j in range(nbk):
                                ic = i0 // 128 + j
                                r = eng.tensor_copy(
                                    hqT[:, (tg * ICH + ic) * 128:
                                        (tg * ICH + ic + 1) * 128],
                                    v[:, j * 128:(j + 1) * 128])
                            r.then_inc(s_hev, 1)
                            return r
                        ths.append((evac, False))
                        il.run([("h", ths)])
                for tg in range(TOKGRP):
                    dn = G * TOKGRP + tg
                    for half in range(2):
                        eng.wait_ge(s_dn, 2 * dn + half + 1)
                        eng.wait_ge(s_trl, 16 * (dn + 1))
                        if dn >= 2 and half == 0:
                            eng.wait_ge(s_out, 16 * (dn - 1))
                        r = eng.tensor_tensor(
                            yt[:, (tg % 2) * E + half * 512:
                               (tg % 2) * E + (half + 1) * 512],
                            trl[:, (tg % 2) * E + half * 512:
                                (tg % 2) * E + (half + 1) * 512],
                            PB[6 + half][:], ALU.add)
                        r.then_inc(s_y, 1)

        @blk.gpsimd
        def _(eng):
            for G in range(NGRP):
                for tg in range(TOKGRP):
                    t = G * TOKGRP + tg
                    dn = G * TOKGRP + tg
                    eng.wait_ge(s_ts, 16 * (t + 1))
                    if dn >= 2:
                        eng.wait_ge(s_y, 2 * (dn - 2) + 2)
                    eng.dma_start(trl[:, (tg % 2) * E:(tg % 2 + 1) * E],
                                  ts_s[t * 128:(t + 1) * 128, :]
                                  ).then_inc(s_trl, 16)
                    eng.wait_ge(s_y, 2 * dn + 2)
                    eng.dma_start(y_d[t * 128:(t + 1) * 128, :],
                                  yt[:, (tg % 2) * E:(tg % 2 + 1) * E]
                                  ).then_inc(s_out, 16)
            eng.wait_ge(s_out, 16 * NT)
            pairs = [("xqT", xqT), ("lam", lam)]
            for name, tile in pairs:
                if name in dbg:
                    eng.dma_start(dbg[name][:, :], tile[:]).then_inc(s_dbg, 16)
                    ndbg[0] += 1
            drams = [("tattn", ts_s), ("wgq", wgq_s), ("wuq", wuq_s),
                     ("wdq", wdq_s)]
            for name, dt_ in drams:
                if name in dbg:
                    eng.dma_start(dbg[name][:, :], dt_[:, :]).then_inc(s_dbg, 16)
                    ndbg[0] += 1
            if ndbg[0]:
                eng.wait_ge(s_dbg, 16 * ndbg[0])

        ctxC.close()

    return nc, dbg


def kernel(**inputs):
    x_full = np.asarray(inputs["L0_attn_x"], dtype=np.float32)
    base = {
        "wq": np.ascontiguousarray(inputs["L0_attn_wq"], dtype=np.float32),
        "wk": np.ascontiguousarray(inputs["L0_attn_wk"], dtype=np.float32),
        "wv": np.ascontiguousarray(inputs["L0_attn_wv"], dtype=np.float32),
        "wo": np.ascontiguousarray(inputs["L0_attn_wo"], dtype=np.float32),
        "lq1": np.ascontiguousarray(inputs["L0_attn_lambda_q1"], dtype=np.float32),
        "lk1": np.ascontiguousarray(inputs["L0_attn_lambda_k1"], dtype=np.float32),
        "lq2": np.ascontiguousarray(inputs["L0_attn_lambda_q2"], dtype=np.float32),
        "lk2": np.ascontiguousarray(inputs["L0_attn_lambda_k2"], dtype=np.float32),
        "g": np.ascontiguousarray(inputs["L0_attn_g"], dtype=np.float32),
        "wg": np.ascontiguousarray(inputs["L0_ffn_w_gate"], dtype=np.float32),
        "wu": np.ascontiguousarray(inputs["L0_ffn_w_up"], dtype=np.float32),
        "wd": np.ascontiguousarray(inputs["L0_ffn_w_down"], dtype=np.float32),
    }
    nc, dbg = build_program(debug=_DEBUG)
    in_maps = []
    for c in range(8):
        m = dict(base)
        m["x"] = np.ascontiguousarray(x_full[c])
        in_maps.append(m)
    res = run_bass_kernel_spmd(nc, in_maps, core_ids=list(range(8)))
    out = np.stack([np.asarray(res.results[c]["y"], dtype=np.float32)
                    for c in range(8)], axis=0)
    if _DEBUG:
        kernel.last_results = res
    return out


# version 10
# speedup vs baseline: 1.0028x; 1.0028x over previous
import math
from contextlib import ExitStack

import numpy as np
import ml_dtypes

import concourse.bass as bass
import concourse.mybir as mybir
from concourse.bass_utils import run_bass_kernel_spmd

F32 = mybir.dt.float32
BF16 = mybir.dt.bfloat16
I32 = mybir.dt.int32
AF = mybir.ActivationFunctionType
ALU = mybir.AluOpType
AX = mybir.AxisListType.X

T = 2048
E = 1024
H = 8
D = 64
II = 2816
NT = T // 128
NQB = 4
KCH = E // 128
ICH = II // 128
NHEADS = NQB * H
LAMBDA_INIT = 0.8 - 0.6 * math.exp(-0.3 * 1)
C1 = 6291456.0
NEG_BIG = -1e30
P40 = float(2.0 ** 40)
IBLK = [512, 512, 512, 512, 512, 256]
IB_OFF = [0, 512, 1024, 1536, 2048, 2560]
NIB = len(IBLK)
TOKGRP = 8
NGRP = NT // TOKGRP
NWT = 52
NWCHUNK = 60
WW = 1408

_DEBUG = set()


def quant_thunks(eng, src, dst, ws, F):
    nb = F // 16
    r3s = src[:, :F].rearrange("p (b k) -> p b k", k=16)
    r3d = dst[:, :F].rearrange("p (b k) -> p b k", k=16)
    r3t = ws["tmp"][:, :F].rearrange("p (b k) -> p b k", k=16)
    r3a = ws["a"][:, :F].rearrange("p (b k) -> p b k", k=16)
    th = []
    th.append((lambda: eng.tensor_reduce(
        ws["bm"][:, :nb], r3s, AX, ALU.max, apply_absolute_value=True), False))
    th.append((lambda: eng.tensor_scalar(
        ws["sgn"][:, :F].bitcast(I32), src[:, :F].bitcast(I32),
        -0x80000000, 0x3F800000, ALU.bitwise_and, ALU.bitwise_or), False))
    th.append((lambda: eng.tensor_scalar(
        ws["scale"][:, :nb], ws["bm"][:, :nb], 1.0 / 6.0, 1e-12,
        ALU.mult, ALU.max), True))
    th.append((lambda: eng.tensor_scalar(
        ws["a"][:, :F].bitcast(I32), src[:, :F].bitcast(I32),
        0x7FFFFFFF, None, ALU.bitwise_and), False))
    th.append((lambda: eng.reciprocal(ws["rs"][:, :nb], ws["scale"][:, :nb]),
               True))
    th.append((lambda: eng.tensor_tensor(
        r3a, r3a, ws["rs"][:, :nb, None].broadcast_to([128, nb, 16]),
        ALU.mult), True))
    th.append((lambda: eng.tensor_scalar(
        ws["lo"][:, :F], ws["a"][:, :F], 1.0, C1, ALU.min, ALU.add), True))
    th.append((lambda: eng.tensor_scalar(
        ws["hi"][:, :F], ws["a"][:, :F], 1.0, 6.0, ALU.max, ALU.min), False))
    th.append((lambda: eng.tensor_scalar(
        ws["lo"][:, :F], ws["lo"][:, :F], C1 + 1.0, None, ALU.subtract), True))
    th.append((lambda: eng.tensor_scalar(
        ws["hi"][:, :F].bitcast(I32), ws["hi"][:, :F].bitcast(I32),
        21, None, ALU.logical_shift_right), True))
    th.append((lambda: eng.tensor_scalar(
        ws["hi"][:, :F].bitcast(I32), ws["hi"][:, :F].bitcast(I32),
        1.0, None, ALU.add), True))
    th.append((lambda: eng.tensor_scalar(
        ws["hi"][:, :F].bitcast(I32), ws["hi"][:, :F].bitcast(I32),
        1, None, ALU.logical_shift_right), True))
    th.append((lambda: eng.tensor_scalar(
        ws["hi"][:, :F].bitcast(I32), ws["hi"][:, :F].bitcast(I32),
        22, None, ALU.logical_shift_left), True))
    th.append((lambda: eng.tensor_tensor(
        ws["tmp"][:, :F], ws["lo"][:, :F], ws["hi"][:, :F], ALU.add), True))
    th.append((lambda: eng.tensor_tensor(
        ws["tmp"][:, :F], ws["tmp"][:, :F], ws["sgn"][:, :F], ALU.mult), True))
    th.append((lambda: eng.tensor_tensor(
        r3d, r3t, ws["scale"][:, :nb, None].broadcast_to([128, nb, 16]),
        ALU.mult), True))
    return th


class VecEmitter:
    def __init__(self, eng, dummy):
        self.eng = eng
        self.dummy = dummy
        self.last = None

    def fence(self):
        self.eng.tensor_scalar(self.dummy[:, 0:2], self.dummy[:, 0:2],
                               1.0, None, ALU.mult)
        self.last = "_d"

    def emit(self, sid, thunk, needs_fence):
        if needs_fence == "w":
            return thunk()
        if needs_fence and self.last == sid:
            self.fence()
        r = thunk()
        self.last = sid
        return r

    def run(self, streams):
        idx = [0] * len(streams)
        while True:
            done = 0
            for si, (sid, ths) in enumerate(streams):
                if idx[si] >= len(ths):
                    done += 1
                    continue
                t, f = ths[idx[si]]
                self.emit(sid, t, f)
                idx[si] += 1
            if done == len(streams):
                break


def ns_of(qb):
    return 4 * qb + 4


def sc_base(qb, h):
    return sum(2 * ns_of(q) * H for q in range(qb)) + h * 2 * ns_of(qb)


def pv_base(qb, h):
    return sum(ns_of(q) * H for q in range(qb)) + h * ns_of(qb)


def mk_base(qb, h):
    return (qb * H + h) * 8


SC_TOT = sum(2 * ns_of(q) * H for q in range(NQB))
PV_TOT = sum(ns_of(q) * H for q in range(NQB))
BCB = 48
BC_PER = 80


def bc_base(hi):
    return BCB + hi * BC_PER


def build_program(debug=(), max_phase=4, p3_qb=NQB, p3_h=H, p3_stage=9):
    nc = bass.Bass("TRN2", target_bir_lowering=False)

    x_d = nc.dram_tensor("x", [T, E], F32, kind="ExternalInput")
    wq_d = nc.dram_tensor("wq", [E, E], F32, kind="ExternalInput")
    wk_d = nc.dram_tensor("wk", [E, E], F32, kind="ExternalInput")
    wv_d = nc.dram_tensor("wv", [E, E], F32, kind="ExternalInput")
    wo_d = nc.dram_tensor("wo", [E, E], F32, kind="ExternalInput")
    lq1_d = nc.dram_tensor("lq1", [D], F32, kind="ExternalInput")
    lk1_d = nc.dram_tensor("lk1", [D], F32, kind="ExternalInput")
    lq2_d = nc.dram_tensor("lq2", [D], F32, kind="ExternalInput")
    lk2_d = nc.dram_tensor("lk2", [D], F32, kind="ExternalInput")
    g_d = nc.dram_tensor("g", [128], F32, kind="ExternalInput")
    wg_d = nc.dram_tensor("wg", [II, E], F32, kind="ExternalInput")
    wu_d = nc.dram_tensor("wu", [II, E], F32, kind="ExternalInput")
    wd_d = nc.dram_tensor("wd", [E, II], F32, kind="ExternalInput")

    y_d = nc.dram_tensor("y", [T, E], F32, kind="ExternalOutput")

    wgq_s = nc.dram_tensor("wgq_s", [II, E], BF16, kind="Internal")
    wuq_s = nc.dram_tensor("wuq_s", [II, E], BF16, kind="Internal")
    wdq_s = nc.dram_tensor("wdq_s", [E, II], BF16, kind="Internal")
    ts_s = nc.dram_tensor("ts_s", [T, E], F32, kind="Internal")
    bc_s = nc.dram_tensor("bc_s", [8, 512], F32, kind="Internal")
    cc_s = nc.dram_tensor("cc_s", [32], F32, kind="Internal")

    dbg = {}
    ndbg = [0]

    def dbg_t(name, shape, dt=F32):
        if name in debug:
            dbg[name] = nc.dram_tensor("dbg_" + name, shape, dt,
                                       kind="ExternalOutput")
        return dbg.get(name)

    dbg_t("xnT", [128, KCH * T], BF16)
    dbg_t("xqT", [128, KCH * T], BF16)
    dbg_t("QT", [128, H * T], BF16)
    dbg_t("KT", [128, H * T], BF16)
    dbg_t("V", [128, NT * E], BF16)
    dbg_t("bias16", [128, 16], F32)
    dbg_t("tattn", [T, E], F32)
    dbg_t("wgq", [II, E], BF16)
    dbg_t("wuq", [II, E], BF16)
    dbg_t("wdq", [E, II], BF16)
    dbg_t("lam", [1, 8], F32)
    dbg_t("p3trace", [NHEADS, 5 * 512], F32)
    dbg_t("arch", [128, 2 * 512], F32)
    dbg_t("aTd", [128, H * 512 * NQB], BF16)
    dbg_t("pvd", [128, 4 * 1024], F32)

    ctxG = ExitStack()
    ctxA1 = ExitStack()
    ctxA2 = ExitStack()
    ctxB = ExitStack()
    ctxBw = ExitStack()
    ctxD = ExitStack()
    ctxC = ExitStack()
    with ExitStack() as ctx:
        def mk_sb(stack):
            return lambda n, sh, dt, side="left": stack.enter_context(
                nc.sbuf_tensor(n, sh, dt, side=side))
        sbG = mk_sb(ctxG)
        sbA1 = mk_sb(ctxA1)
        sbA2 = mk_sb(ctxA2)
        sbB = mk_sb(ctxB)
        sbBw = mk_sb(ctxBw)
        sbD = mk_sb(ctxD)
        sbC = mk_sb(ctxC)
        pst = lambda n, sh, dt: ctx.enter_context(nc.psum_tensor(n, sh, dt))
        sem = lambda n: ctx.enter_context(nc.semaphore(n))
        ctx.callback(ctxG.close)

        xqT = sbG("xqT", [128, KCH * T], BF16)
        iden = sbG("iden", [128, 128], BF16)
        idenf = sbG("idenf", [128, 128], F32)
        ones_b = sbG("ones_b", [128, 1], BF16)
        ones_f = sbG("ones_f", [128, 1], F32)
        lam = sbG("lam", [1, 8], F32)
        lam40 = sbG("lam40", [128, 1], F32)
        gp = sbG("gp", [128, 1], F32)
        cm20 = sbG("cm20", [128, 1], F32)
        bias16 = sbG("bias16", [128, 16], F32)
        vdum = sbG("vdum", [128, 4], F32)
        adum = sbG("adum", [128, 4], F32)
        lnS = sbG("lnS", [128, 14], F32)

        def mkws(p, wide, alloc):
            return {
                "bm": alloc(p + "bm", [128, wide // 16], F32),
                "scale": alloc(p + "scale", [128, wide // 16], F32),
                "rs": alloc(p + "rs", [128, wide // 16], F32),
                "a": alloc(p + "a", [128, wide], F32),
                "lo": alloc(p + "lo", [128, wide], F32),
                "hi": alloc(p + "hi", [128, wide], F32),
                "sgn": alloc(p + "sgn", [128, wide], F32),
                "tmp": alloc(p + "tmp", [128, wide], F32),
            }
        wsx = mkws("wsx_", E, sbA1)
        wsw = mkws("wsw_", WW, sbA1)
        xin = sbA1("xin", [128, 2 * E], F32)
        xnf = sbA1("xnf", [128, 2 * E], F32)
        xnb = sbA1("xnb", [128, 2 * E], BF16)
        xqb = sbA1("xqb", [128, 2 * E], BF16)
        xsqd = sbA1("xsqd", [128, E], F32)
        wqin = sbA1("wqin", [128, 2 * WW], F32)
        wqout = sbA1("wqout", [128, 2 * WW], BF16)
        iotaD = sbA1("iotaD", [128, 512], F32)
        lvec = sbA1("lvec", [1, 4 * D], F32)
        lprod = sbA1("lprod", [1, 2 * D], F32)

        xnT = sbA2("xnT", [128, KCH * T], BF16, side="right")

        PB = [pst(f"pb{i}", [128, 512], F32) for i in range(8)]

        s_xin = sem("s_xin")
        s_ln = sem("s_ln")
        s_act = sem("s_act")
        s_tr = sem("s_tr")
        s_ev = sem("s_ev")
        s_q = sem("s_q")
        s_wqi = sem("s_wqi")
        s_wqd = sem("s_wqd")
        s_wqo = sem("s_wqo")
        s_w2 = sem("s_w2")
        s_w2c = sem("s_w2c")
        s_mm2 = sem("s_mm2")
        s_ev2 = sem("s_ev2")
        s_cb = sem("s_cb")
        s_cbt = sem("s_cbt")
        s_cbd = sem("s_cbd")
        s_cbq = sem("s_cbq")
        s_cbm = sem("s_cbm")
        s_cbe = sem("s_cbe")
        s_sc = sem("s_sc")
        s_mk = sem("s_mk")
        s_exp = sem("s_exp")
        s_pv = sem("s_pv")
        s_pev = sem("s_pev")
        s_epv = sem("s_epv")
        s_epa = sem("s_epa")
        s_bc = sem("s_bc")
        s_aT = sem("s_aT")
        s_rms = sem("s_rms")
        s_wo = sem("s_wo")
        s_t4 = sem("s_t4")
        s_ts = sem("s_ts")
        s_xrl = sem("s_xrl")
        s_p0 = sem("s_p0")
        s_wdT = sem("s_wdT")
        s_wgT = sem("s_wgT")
        s_gu = sem("s_gu")
        s_si = sem("s_si")
        s_h = sem("s_h")
        s_hq = sem("s_hq")
        s_htr = sem("s_htr")
        s_hev = sem("s_hev")
        s_dn = sem("s_dn")
        s_trl = sem("s_trl")
        s_y = sem("s_y")
        s_out = sem("s_out")
        s_dbg = sem("s_dbg")
        s_af = sem("s_af")
        af_n = [0]

        def afence(eng, r):
            af_n[0] += 1
            r.then_inc(s_af, 1)
            eng.wait_ge(s_af, af_n[0])

        blk = ctx.enter_context(nc.Block())

        def dstage(hi, k):
            return hi * 5 + k

        def astage(hi, k):
            return hi * 4 + k

        @blk.sync
        def _(eng):
            eng.dma_start(lvec[0:1, 0:D], lq1_d[None, :]).then_inc(s_p0, 16)
            eng.dma_start(lvec[0:1, D:2 * D], lk1_d[None, :]).then_inc(s_p0, 16)
            eng.dma_start(lvec[0:1, 2 * D:3 * D], lq2_d[None, :]).then_inc(s_p0, 16)
            eng.dma_start(lvec[0:1, 3 * D:4 * D], lk2_d[None, :]).then_inc(s_p0, 16)
            eng.dma_start(gp[:, 0:1], g_d[:, None]).then_inc(s_p0, 16)

        @blk.gpsimd
        def _(eng):
            eng.iota(iotaD[:], [[1, 512]], channel_multiplier=-1,
                     allow_small_or_imprecise_dtypes=True).then_inc(s_p0, 1)

        @blk.vector
        def _(eng):
            eng.wait_ge(s_p0, 81)
            eng.memset(ones_b[:], 1.0)
            eng.memset(ones_f[:], 1.0)
            eng.memset(vdum[:], 1.0)
            eng.memset(cm20[:], -20.0 * math.log(2.0))
            eng.memset(adum[:], 1.0)
            eng.tensor_scalar(iden[:], iotaD[:, 0:128], 0.0, None, ALU.is_equal)
            eng.tensor_scalar(idenf[:, :], iotaD[:, 0:128], 0.0, None,
                              ALU.is_equal)
            eng.tensor_scalar(gp[:, 0:1], gp[:, 0:1], 1.0 - LAMBDA_INIT,
                              None, ALU.mult)
            eng.tensor_tensor(lprod[0:1, 0:D], lvec[0:1, 0:D],
                              lvec[0:1, D:2 * D], ALU.mult)
            eng.tensor_tensor(lprod[0:1, D:2 * D], lvec[0:1, 2 * D:3 * D],
                              lvec[0:1, 3 * D:4 * D], ALU.mult)
            eng.reduce_sum(lam[0:1, 0:2],
                           lprod[0:1, :].rearrange("p (b k) -> p b k", k=D), AX)
            eng.tensor_scalar(vdum[:, 0:2], vdum[:, 0:2], 1.0, None, ALU.mult)
            eng.tensor_scalar(lam[0:1, 3:5], lam[0:1, 0:2], 1.0,
                              None, ALU.mult).then_inc(s_ln, 1)

        @blk.scalar
        def _(eng):
            eng.wait_ge(s_ln, 1)
            eng.activation(lam[0:1, 5:7], lam[0:1, 3:5],
                           AF.Exp).then_inc(s_act, 1)

        @blk.vector
        def _(eng):
            eng.wait_ge(s_act, 1)
            eng.tensor_tensor(lam[0:1, 2:3], lam[0:1, 5:6], lam[0:1, 6:7],
                              ALU.subtract)
            eng.tensor_scalar(vdum[:, 0:2], vdum[:, 0:2], 1.0, None, ALU.mult)
            eng.tensor_scalar(lam[0:1, 2:3], lam[0:1, 2:3], LAMBDA_INIT,
                              None, ALU.add)
            eng.tensor_scalar(vdum[:, 0:2], vdum[:, 0:2], 1.0, None, ALU.mult)
            eng.tensor_scalar(lam[0:1, 7:8], lam[0:1, 2:3], P40,
                              None, ALU.mult).then_inc(s_ln, 1)

        @blk.gpsimd
        def _(eng):
            eng.wait_ge(s_ln, 2)
            eng.dma_start(cc_s[16:17], lam[0:1, 7:8]).then_inc(s_p0, 16)
            eng.wait_ge(s_p0, 97)
            eng.dma_start(lam40[:, 0:1],
                          cc_s[16:17, None].broadcast_to([128, 1])
                          ).then_inc(s_p0, 16)

        def wt_src(i):
            if i < 22:
                return wg_d[i * 128:(i + 1) * 128, :], E, \
                    wgq_s[i * 128:(i + 1) * 128, :]
            if i < 44:
                j = i - 22
                return wu_d[j * 128:(j + 1) * 128, :], E, \
                    wuq_s[j * 128:(j + 1) * 128, :]
            j, half = divmod(i - 44, 2)
            c0 = half * WW
            return wd_d[j * 128:(j + 1) * 128, c0:c0 + WW], WW, \
                wdq_s[j * 128:(j + 1) * 128, c0:c0 + WW]

        @blk.sync
        def _(eng):
            for t in range(NT):
                if t >= 2:
                    eng.wait_ge(s_q, t - 1)
                    eng.wait_ge(s_act, 1 + 3 * (t - 1))
                eng.dma_start(xin[:, (t % 2) * E:(t % 2 + 1) * E],
                              x_d[t * 128:(t + 1) * 128, :]).then_inc(s_xin, 16)

        @blk.gpsimd
        def _(eng):
            for i in range(NWCHUNK):
                src, F, dst = wt_src(i)
                if i >= 2:
                    eng.wait_ge(s_wqd, i - 1)
                eng.dma_start(wqin[:, (i % 2) * WW:(i % 2) * WW + F],
                              src).then_inc(s_wqi, 16)
                if i >= 1:
                    eng.wait_ge(s_wqd, i)
                    psrc, pF, pdst = wt_src(i - 1)
                    eng.dma_start(pdst, wqout[:, ((i - 1) % 2) * WW:
                                               ((i - 1) % 2) * WW + pF]
                                  ).then_inc(s_wqo, 16)
            eng.wait_ge(s_wqd, NWCHUNK)
            psrc, pF, pdst = wt_src(NWCHUNK - 1)
            eng.dma_start(pdst, wqout[:, ((NWCHUNK - 1) % 2) * WW:
                                       ((NWCHUNK - 1) % 2) * WW + pF]
                          ).then_inc(s_wqo, 16)

        @blk.scalar
        def _(eng):
            for t in range(NT):
                p = t % 2
                eng.wait_ge(s_xin, 16 * (t + 1))
                eng.activation(xsqd[:, :], xin[:, p * E:(p + 1) * E], AF.Square,
                               accum_out=lnS[:, p * 7 + 1:p * 7 + 2]
                               ).then_inc(s_act, 1)
                eng.wait_ge(s_ln, 2 + 3 * t + 1)
                r = eng.activation(lnS[:, p * 7 + 5:p * 7 + 6],
                                   lnS[:, p * 7 + 4:p * 7 + 5], AF.Ln)
                afence(eng, r)
                eng.activation(lnS[:, p * 7 + 5:p * 7 + 6],
                               lnS[:, p * 7 + 5:p * 7 + 6], AF.Exp,
                               scale=-0.5).then_inc(s_act, 1)
                eng.wait_ge(s_ln, 2 + 3 * t + 2)
                if t >= 2:
                    eng.wait_ge(s_q, t - 1)
                eng.activation(xnf[:, p * E:(p + 1) * E],
                               xin[:, p * E:(p + 1) * E], AF.Identity,
                               bias=lnS[:, p * 7 + 6:p * 7 + 7],
                               scale=lnS[:, p * 7 + 5:p * 7 + 6]
                               ).then_inc(s_act, 1)

        @blk.vector
        def _(eng):
            il = VecEmitter(eng, vdum)
            wtile = [0]

            def weight_stream(nmax):
                ths = []
                while wtile[0] < min(nmax, NWCHUNK):
                    i = wtile[0]
                    src, F, dst = wt_src(i)
                    pp = (i % 2) * WW
                    ths.append((lambda i=i: eng.wait_ge(s_wqi, 16 * (i + 1)),
                                "w"))
                    if i >= 2:
                        ths.append((lambda i=i: eng.wait_ge(s_wqo,
                                                            16 * (i - 1)),
                                    "w"))
                    qt = quant_thunks(eng, wqin[:, pp:pp + F],
                                      wqout[:, pp:pp + F], wsw, F)
                    last_t, last_f = qt[-1]

                    def fin(last_t=last_t):
                        r = last_t()
                        r.then_inc(s_wqd, 1)
                        return r
                    qt[-1] = (fin, last_f)
                    ths.extend(qt)
                    wtile[0] += 1
                return ths

            for t in range(NT):
                p = t % 2
                xth = []
                xth.append((lambda t=t: eng.wait_ge(s_xin, 16 * (t + 1)),
                            "w"))
                xth.append((lambda p=p: eng.reduce_sum(
                    lnS[:, p * 7 + 0:p * 7 + 1],
                    xin[:, p * E:(p + 1) * E], AX), False))
                xth.append((lambda p=p: eng.tensor_scalar(
                    lnS[:, p * 7 + 2:p * 7 + 3], lnS[:, p * 7 + 0:p * 7 + 1],
                    1.0 / 1024.0, None, ALU.mult), True))
                xth.append((lambda p=p: eng.tensor_tensor(
                    lnS[:, p * 7 + 3:p * 7 + 4], lnS[:, p * 7 + 2:p * 7 + 3],
                    lnS[:, p * 7 + 2:p * 7 + 3], ALU.mult), True))
                xth.append((lambda t=t: eng.wait_ge(s_act, 1 + 3 * t + 1),
                            "w"))
                xth.append((lambda p=p: eng.scalar_tensor_tensor(
                    lnS[:, p * 7 + 4:p * 7 + 5], lnS[:, p * 7 + 1:p * 7 + 2],
                    1.0 / 1024.0, lnS[:, p * 7 + 3:p * 7 + 4],
                    ALU.mult, ALU.subtract), True))

                def vareps(p=p):
                    r = eng.tensor_scalar(lnS[:, p * 7 + 4:p * 7 + 5],
                                          lnS[:, p * 7 + 4:p * 7 + 5],
                                          1e-5, None, ALU.add)
                    r.then_inc(s_ln, 1)
                    return r
                xth.append((vareps, True))
                xth.append((lambda t=t: eng.wait_ge(s_act, 1 + 3 * t + 2),
                            "w"))

                def negm(p=p):
                    r = eng.scalar_tensor_tensor(
                        lnS[:, p * 7 + 6:p * 7 + 7],
                        lnS[:, p * 7 + 2:p * 7 + 3], -1.0,
                        lnS[:, p * 7 + 5:p * 7 + 6], ALU.mult, ALU.mult)
                    r.then_inc(s_ln, 1)
                    return r
                xth.append((negm, True))
                xth.append((lambda t=t: eng.wait_ge(s_act, 1 + 3 * t + 3),
                            "w"))
                if t >= 2:
                    xth.append((lambda t=t: eng.wait_ge(s_tr, 2 * (t - 2) + 1),
                                "w"))

                def xnbcp(p=p):
                    r = eng.tensor_copy(xnb[:, p * E:(p + 1) * E],
                                        xnf[:, p * E:(p + 1) * E])
                    r.then_inc(s_ln, 1)
                    return r
                xth.append((xnbcp, True))
                if t >= 2:
                    xth.append((lambda t=t: eng.wait_ge(s_tr, 2 * (t - 2) + 2),
                                "w"))
                qt = quant_thunks(eng, xnf[:, p * E:(p + 1) * E],
                                  xqb[:, p * E:(p + 1) * E], wsx, E)
                last_t, last_f = qt[-1]

                def qfin(last_t=last_t):
                    r = last_t()
                    r.then_inc(s_q, 1)
                    return r
                qt[-1] = (qfin, last_f)
                xth.extend(qt)

                def evac_xn(t=t):
                    eng.wait_ge(s_tr, 2 * t + 1)
                    v = PB[t % 2][:].bitcast(BF16)
                    for j in range(KCH):
                        r = eng.tensor_copy(
                            xnT[:, j * T + t * 128:j * T + (t + 1) * 128],
                            v[:, j * 128:(j + 1) * 128])
                    r.then_inc(s_ev, 1)
                    return r
                xth.append((evac_xn, False))

                def evac_xq(t=t):
                    eng.wait_ge(s_tr, 2 * t + 2)
                    v = PB[2 + t % 2][:].bitcast(BF16)
                    for j in range(KCH):
                        r = eng.tensor_copy(
                            xqT[:, j * T + t * 128:j * T + (t + 1) * 128],
                            v[:, j * 128:(j + 1) * 128])
                    r.then_inc(s_ev, 1)
                    return r
                xth.append((evac_xq, False))

                wth = weight_stream(4 * (t + 1) if t < NT - 1 else NWCHUNK)
                il.run([("x", xth), ("w", wth)])

        @blk.tensor
        def _(eng):
            for t in range(NT):
                p = t % 2
                eng.wait_ge(s_ln, 2 + 3 * t + 3)
                if t >= 2:
                    eng.wait_ge(s_ev, 2 * (t - 2) + 1)
                v = PB[p][:].bitcast(BF16)
                for j in range(KCH):
                    r = eng.transpose(v[:, j * 128:(j + 1) * 128],
                                      xnb[:, p * E + j * 128:
                                          p * E + (j + 1) * 128], iden[:])
                r.then_inc(s_tr, 1)
                eng.wait_ge(s_q, t + 1)
                if t >= 2:
                    eng.wait_ge(s_ev, 2 * (t - 2) + 2)
                v = PB[2 + p][:].bitcast(BF16)
                for j in range(KCH):
                    r = eng.transpose(v[:, j * 128:(j + 1) * 128],
                                      xqb[:, p * E + j * 128:
                                          p * E + (j + 1) * 128], iden[:])
                r.then_inc(s_tr, 1)


        def _finisher(waits, dumps):
            @blk.gpsimd
            def _(eng):
                for s, v in waits:
                    eng.wait_ge(s, v)
                nd = 0
                for name, ap in dumps:
                    if name in dbg:
                        eng.dma_start(dbg[name][:, :], ap).then_inc(s_dbg, 16)
                        nd += 1
                for t in range(NT):
                    eng.dma_start(y_d[t * 128:(t + 1) * 128, :],
                                  wqin[:, 0:E] if max_phase == 1 else
                                  wpf[:, 0:E]).then_inc(s_out, 16)
                eng.wait_ge(s_out, 16 * NT)
                nd += ndbg[0]
                if nd:
                    eng.wait_ge(s_dbg, 16 * nd)

        if max_phase <= 1:
            _finisher([(s_q, NT), (s_ev, 2 * NT), (s_tr, 2 * NT),
                       (s_act, 1 + 3 * NT), (s_wqd, NWCHUNK),
                       (s_wqo, 16 * NWCHUNK), (s_ln, 2 + 3 * NT),
                       (s_p0, 113)],
                      [("xnT", xnT[:]), ("xqT", xqT[:]), ("lam", lam[:]),
                       ("wgq", wgq_s[:, :]), ("wuq", wuq_s[:, :]),
                       ("wdq", wdq_s[:, :])])
            ctxA1.close()
            ctxA2.close()
            return nc, dbg

        ctxA1.close()
        QT = sbB("QT", [128, H * T], BF16)
        KT = sbB("KT", [128, H * T], BF16)
        V = sbB("V", [128, NT * E], BF16)
        wprojb = sbB("wprojb", [128, KCH * E], BF16)
        wob = wprojb
        wpf = sbBw("wpf", [128, 2 * E], F32)
        cbsq = sbBw("cbsq", [128, 2 * 512], BF16)
        cbev = sbBw("cbev", [128, 2 * 512], F32)
        qn2max = sbBw("qn2max", [128, 32], F32)
        qn2t = sbBw("qn2t", [32, 128], F32)
        cbs = sbBw("cbs", [32, 4], F32)

        if "xnT" in dbg:
            @blk.gpsimd
            def _(eng):
                eng.wait_ge(s_ev, 2 * NT)
                eng.dma_start(dbg["xnT"][:, :], xnT[:]).then_inc(s_dbg, 16)
                ndbg[0] += 1

        @blk.sync
        def _(eng):
            eng.wait_ge(s_q, NT)
            eng.wait_ge(s_wqd, NWCHUNK)
            eng.wait_ge(s_wqo, 16 * NWCHUNK)
            eng.wait_ge(s_ev, 2 * NT)
            eng.wait_ge(s_tr, 2 * NT)
            eng.wait_ge(s_act, 1 + 3 * NT)
            for pi in range(4):
                src = [wq_d, wk_d, wv_d, wo_d][pi]
                for k in range(KCH):
                    idx = pi * KCH + k
                    if idx >= 2:
                        eng.wait_ge(s_w2c, idx - 1)
                    eng.dma_start(wpf[:, (idx % 2) * E:(idx % 2 + 1) * E],
                                  src[k * 128:(k + 1) * 128, :]
                                  ).then_inc(s_w2, 16)

        @blk.vector
        def _(eng):
            for pi in range(4):
                for k in range(KCH):
                    idx = pi * KCH + k
                    eng.wait_ge(s_w2, 16 * (idx + 1))
                    if pi in (1, 2) and k == 0:
                        eng.wait_ge(s_mm2, 32 * pi)
                    if pi == 3 and k == 0:
                        eng.wait_ge(s_mm2, 96)
                    dst = wprojb
                    r = eng.tensor_copy(dst[:, k * E:(k + 1) * E],
                                        wpf[:, (idx % 2) * E:(idx % 2 + 1) * E])
                    r.then_inc(s_w2c, 1)

        @blk.tensor
        def _(eng):
            n = 0
            for pi in range(2):
                eng.wait_ge(s_w2c, (pi + 1) * KCH)
                eng.wait_ge(s_ev, 2 * NT)
                for d in range(KCH):
                    for tb in range(4):
                        bank = PB[n % 4]
                        if n >= 4:
                            eng.wait_ge(s_ev2, n - 4 + 1)
                        for k in range(KCH):
                            r = eng.matmul(
                                bank[:],
                                wprojb[:, k * E + d * 128:k * E + (d + 1) * 128],
                                xnT[:, k * T + tb * 512:k * T + (tb + 1) * 512],
                                start=(k == 0), stop=(k == KCH - 1))
                        r.then_inc(s_mm2, 1)
                        n += 1
            eng.wait_ge(s_w2c, 3 * KCH)
            for t in range(NT):
                for half in range(2):
                    bank = PB[4 + n % 4]
                    if n >= 68:
                        eng.wait_ge(s_ev2, n - 4 + 1)
                    for k in range(KCH):
                        r = eng.matmul(
                            bank[:],
                            xnT[:, k * T + t * 128:k * T + (t + 1) * 128],
                            wprojb[:, k * E + half * 512:k * E + (half + 1) * 512],
                            start=(k == 0), stop=(k == KCH - 1))
                    r.then_inc(s_mm2, 1)
                    n += 1

        @blk.scalar
        def _(eng):
            n = 0
            for pi in range(2):
                dstT = [QT, KT][pi]
                for d in range(KCH):
                    for tb in range(4):
                        eng.wait_ge(s_mm2, n + 1)
                        r = eng.copy(dstT[:, d * T + tb * 512:
                                          d * T + (tb + 1) * 512],
                                     PB[n % 4][:])
                        r.then_inc(s_ev2, 1)
                        n += 1
            for t in range(NT):
                for half in range(2):
                    eng.wait_ge(s_mm2, n + 1)
                    r = eng.copy(V[:, t * E + half * 512:
                                   t * E + (half + 1) * 512],
                                 PB[4 + n % 4][:])
                    r.then_inc(s_ev2, 1)
                    n += 1

        @blk.scalar
        def _(eng):
            idx = 0
            for pi in range(2):
                srcT = [QT, KT][pi]
                eng.wait_ge(s_ev2, 32 * (pi + 1))
                for hh in range(H):
                    ph = pi * 8 + hh
                    for tb in range(4):
                        if idx >= 2:
                            eng.wait_ge(s_cbm, idx - 1)
                        r = eng.activation(
                            cbsq[:, (tb % 2) * 512:(tb % 2 + 1) * 512],
                            srcT[:, hh * T + tb * 512:hh * T + (tb + 1) * 512],
                            AF.Square)
                        r.then_inc(s_cbq, 1)
                        idx += 1
                    eng.wait_ge(s_cbm, 4 * (ph + 1))
                    eng.copy(cbev[:, 0:512], PB[6][:])
                    r = eng.copy(cbev[:, 512:1024], PB[7][:])
                    r.then_inc(s_cbe, 1)

        @blk.tensor
        def _(eng):
            idx = 0
            for pi in range(2):
                for hh in range(H):
                    ph = pi * 8 + hh
                    if ph >= 1:
                        eng.wait_ge(s_cb, ph)
                    for tb in range(4):
                        eng.wait_ge(s_cbq, idx + 1)
                        for br in range(2):
                            bank = PB[6 + br]
                            row = 32 * tb
                            r = eng.matmul(
                                bank[row:row + 1, :],
                                ones_b[br * 64:(br + 1) * 64, :],
                                cbsq[br * 64:(br + 1) * 64,
                                     (tb % 2) * 512:(tb % 2 + 1) * 512],
                                start=True, stop=True,
                                tile_position=(64 * br, row))
                        r.then_inc(s_cbm, 1)
                        idx += 1
            eng.wait_ge(s_cb, 16)
            eng.wait_ge(s_ev2, 96)
            r = eng.transpose(PB[0][0:32, 0:128], qn2max[:, 0:32], idenf[:])
            r.then_inc(s_cbt, 1)

        @blk.vector
        def _(eng):
            for pi in range(2):
                for hh in range(H):
                    ph = pi * 8 + hh
                    eng.wait_ge(s_cbe, ph + 1)
                    for br in range(2):
                        col = pi * 16 + 2 * hh + br
                        r = eng.reduce_max(
                            qn2max[:, col:col + 1],
                            cbev[:, br * 512:(br + 1) * 512], AX)
                    r.then_inc(s_cb, 1)

        @blk.scalar
        def _(eng):
            eng.wait_ge(s_cbt, 1)
            eng.copy(qn2t[:, :], PB[0][0:32, 0:128]).then_inc(s_cbd, 1)

        @blk.vector
        def _(eng):
            eng.wait_ge(s_cbd, 1)
            r = eng.reduce_max(
                cbs[0:32, 0:1],
                bass.AP(qn2t, 0, [[128, 32], [32, 4]]), AX)
            r.then_inc(s_cb, 1)

        @blk.gpsimd
        def _(eng):
            eng.wait_ge(s_cb, 17)
            eng.dma_start(cbs[0:16, 1:2], cbs[16:32, 0:1]).then_inc(s_bc, 16)

        @blk.vector
        def _(eng):
            eng.wait_ge(s_bc, 16)
            r = eng.tensor_tensor(cbs[0:16, 2:3], cbs[0:16, 0:1],
                                  cbs[0:16, 1:2], ALU.mult)
            r.then_inc(s_cb, 1)

        @blk.scalar
        def _(eng):
            eng.wait_ge(s_cb, 18)
            r = eng.activation(cbs[0:16, 3:4], cbs[0:16, 2:3], AF.Sqrt,
                               scale=(1.01 / 8.0) ** 2)
            r.then_inc(s_cbd, 1)

        @blk.vector
        def _(eng):
            eng.wait_ge(s_cbd, 2)
            r = eng.tensor_scalar(cbs[0:16, 3:4], cbs[0:16, 3:4],
                                  -1.0, -1.0, ALU.mult, ALU.add)
            r.then_inc(s_cb, 1)

        @blk.gpsimd
        def _(eng):
            eng.wait_ge(s_cb, 19)
            eng.dma_start(cc_s[0:16, None], cbs[0:16, 3:4]).then_inc(s_bc, 16)
            eng.wait_ge(s_bc, 32)
            eng.dma_start(bias16[:, :],
                          cc_s[None, 0:16].broadcast_to([128, 16])
                          ).then_inc(s_bc, 16)

        ctxBw.close()
        ctxA2.close()
        ptile = sbD("ptile", [128, 4 * 512], BF16)
        pvsb = sbD("pvsb", [128, 4 * 512], F32)
        apr = sbD("apr", [128, 2 * 512], F32)
        sqa = sbD("sqa", [128, 2 * 512], F32)
        aT = sbD("aT", [128, H * 512], BF16)
        srow = sbD("srow", [128, 5 * 512], F32)
        fbr = sbD("fbr", [128, 2 * 512], F32)
        rstdb = fbr
        xrl = sbD("xrl", [128, E], F32)
        masks = sbD("masks", [128, 4 * 512], F32)
        arch = sbD("arch", [128, 2 * 512], F32) if "arch" in debug else None
        wo_t = sbD("wo_t", [128, E], F32)

        if any(k in dbg for k in ("QT", "KT", "V", "bias16")):
            @blk.gpsimd
            def _(eng):
                eng.wait_ge(s_ev2, 96)
                eng.wait_ge(s_bc, 48)
                for name, tile in (("QT", QT), ("KT", KT), ("V", V),
                                   ("bias16", bias16)):
                    if name in dbg:
                        eng.dma_start(dbg[name][:, :],
                                      tile[:]).then_inc(s_dbg, 16)
                        ndbg[0] += 1

        @blk.tensor
        def _(eng):
            for qb in range(p3_qb):
                ns = ns_of(qb)
                for h in range(p3_h):
                    hi = qb * H + h
                    hpar = hi % 2
                    scb = sc_base(qb, h)
                    pvb = pv_base(qb, h)
                    if hi >= 1 and h >= 1 and p3_stage >= 5:
                        pj = hi - 1
                        ppar = pj % 2
                        eng.wait_ge(s_epa, astage(pj, 2))
                        if pj >= 2:
                            eng.wait_ge(s_epa, astage(pj - 2, 3))
                        rrow = 64 * ppar
                        r = eng.matmul(PB[7][rrow:rrow + 1, :], ones_f[:],
                                       sqa[:, ppar * 512:(ppar + 1) * 512],
                                       start=True, stop=True,
                                       tile_position=(0, rrow))
                        r.then_inc(s_rms, 1)
                    def emit_scores(st):
                        spar = st % 2
                        if hi == 0 and st == 0:
                            eng.wait_ge(s_cbd, 1)
                        if st >= 2:
                            eng.wait_ge(s_exp, scb + 2 * (st - 2) + 2)
                        elif hi > 0:
                            eng.wait_ge(s_exp, scb)
                        if qb > 0 and h == 0 and st < 2:
                            eng.wait_ge(s_t4, qb * 8)
                        r = eng.matmul(PB[2 * spar][:],
                                       KT[0:64, h * T + st * 128:
                                          h * T + (st + 1) * 128],
                                       QT[0:64, h * T + qb * 512:
                                          h * T + (qb + 1) * 512],
                                       start=True, stop=True)
                        r.then_inc(s_sc, 1)
                        r = eng.matmul(PB[2 * spar + 1][:],
                                       KT[64:128, h * T + st * 128:
                                          h * T + (st + 1) * 128],
                                       QT[64:128, h * T + qb * 512:
                                          h * T + (qb + 1) * 512],
                                       start=True, stop=True,
                                       tile_position=(64, 0))
                        r.then_inc(s_sc, 1)

                    def emit_pv(st):
                        spar = st % 2
                        eng.wait_ge(s_exp, scb + 2 * st + 2)
                        if st == 0 and hi >= 1:
                            eng.wait_ge(s_pev, 2 * hi)
                        if st == 0 and hi >= 2:
                            eng.wait_ge(s_epv, dstage(hi - 2, 1))
                        vsl = V[:, st * E + h * 128:st * E + (h + 1) * 128]
                        eng.matmul(PB[4][:], vsl,
                                   ptile[:, (2 * spar) * 512:
                                         (2 * spar + 1) * 512],
                                   start=(st == 0), stop=(st == ns - 1))
                        eng.matmul(PB[5][:], vsl,
                                   ptile[:, (2 * spar + 1) * 512:
                                         (2 * spar + 2) * 512],
                                   start=(st == 0), stop=(st == ns - 1))
                        row1 = 64 * hpar
                        row2 = 64 * hpar + 32
                        eng.matmul(PB[6][row1:row1 + 1, :], ones_b[:],
                                   ptile[:, (2 * spar) * 512:
                                         (2 * spar + 1) * 512],
                                   start=(st == 0), stop=(st == ns - 1),
                                   tile_position=(0, row1))
                        r = eng.matmul(PB[6][row2:row2 + 1, :], ones_b[:],
                                       ptile[:, (2 * spar + 1) * 512:
                                             (2 * spar + 2) * 512],
                                       start=(st == 0), stop=(st == ns - 1),
                                       tile_position=(0, row2))
                        r.then_inc(s_pv, 1)

                    for st in range(ns):
                        emit_scores(st)
                        if p3_stage >= 2 and st >= 2:
                            emit_pv(st - 2)
                    if p3_stage >= 2:
                        emit_pv(ns - 2)
                        emit_pv(ns - 1)
                pj = qb * H + min(p3_h, H) - 1
                ppar = pj % 2
                if p3_stage >= 5:
                    eng.wait_ge(s_epa, astage(pj, 2))
                    if pj >= 2:
                        eng.wait_ge(s_epa, astage(pj - 2, 3))
                    rrow = 64 * ppar
                    r = eng.matmul(PB[7][rrow:rrow + 1, :], ones_f[:],
                                   sqa[:, ppar * 512:(ppar + 1) * 512],
                                   start=True, stop=True,
                                   tile_position=(0, rrow))
                    r.then_inc(s_rms, 1)
                if p3_h < H:
                    continue
                eng.wait_ge(s_aT, (qb + 1) * 8)
                for sub in range(4):
                    for half in range(2):
                        wn = qb * 8 + sub * 2 + half
                        if wn >= 4:
                            eng.wait_ge(s_t4, wn - 3)
                        for hh in range(H):
                            r = eng.matmul(
                                PB[wn % 4][:],
                                aT[:, hh * 512 + sub * 128:
                                   hh * 512 + (sub + 1) * 128],
                                wob[:, hh * E + half * 512:
                                    hh * E + (half + 1) * 512],
                                start=(hh == 0), stop=(hh == H - 1))
                        r.then_inc(s_wo, 1)

        @blk.vector
        def _(eng):
            eng.wait_ge(s_p0, 114)
            for j in range(4):
                eng.tensor_scalar(masks[:, j * 512:(j + 1) * 512],
                                  srow[:, 0:512],
                                  float(128 * j), NEG_BIG, ALU.is_lt, ALU.mult)
            for qb in range(p3_qb):
                ns = ns_of(qb)
                for h in range(p3_h):
                    hi = qb * H + h
                    hpar = hi % 2
                    scb = sc_base(qb, h)
                    r1 = 64 * hpar
                    r2 = 64 * hpar + 32
                    for st in range(ns - 4, ns):
                        j = st - 4 * qb
                        spar = st % 2
                        eng.wait_ge(s_sc, scb + 2 * st + 2)
                        eng.tensor_tensor(PB[2 * spar][:], PB[2 * spar][:],
                                          masks[:, j * 512:(j + 1) * 512],
                                          ALU.add)
                        r = eng.tensor_tensor(PB[2 * spar + 1][:],
                                              PB[2 * spar + 1][:],
                                              masks[:, j * 512:(j + 1) * 512],
                                              ALU.add)
                        r.then_inc(s_mk, 2)
                    if p3_stage < 3:
                        continue
                    eng.wait_ge(s_pv, pv_base(qb, h) + ns)
                    eng.tensor_copy(srow[r1:r1 + 1, 0:512],
                                    PB[6][r1:r1 + 1, :])
                    r = eng.tensor_copy(srow[r2:r2 + 1, 0:512],
                                        PB[6][r2:r2 + 1, :])
                    r.then_inc(s_epv, 1)
                    eng.wait_ge(s_bc, bc_base(hi) + 16)
                    r = eng.tensor_scalar(vdum[:, 2:4], vdum[:, 2:4],
                                          1.0, None, ALU.mult)
                    r.then_inc(s_epv, 1)
                    if p3_stage < 4:
                        continue
                    if hi == 0:
                        eng.wait_ge(s_p0, 113)
                    eng.wait_ge(s_epa, astage(hi, 1))
                    if hi >= 2:
                        eng.wait_ge(s_bc, bc_base(hi - 2) + 32)
                    r = eng.scalar_tensor_tensor(
                        srow[r1:r1 + 1, 2 * 512:3 * 512],
                        srow[r1:r1 + 1, 0:512], lam40[r1:r1 + 1, 0:1],
                        srow[r1:r1 + 1, 512:1024], ALU.mult, ALU.mult)
                    r.then_inc(s_epv, 1)
                    eng.wait_ge(s_bc, bc_base(hi) + 48)
                    eng.wait_ge(s_pev, 2 * hi + 2)
                    eng.tensor_tensor(fbr[:, hpar * 512:(hpar + 1) * 512],
                                      pvsb[:, (2 * hpar + 1) * 512:
                                           (2 * hpar + 2) * 512],
                                      fbr[:, hpar * 512:(hpar + 1) * 512],
                                      ALU.mult)
                    eng.tensor_scalar(vdum[:, 0:2], vdum[:, 0:2], 1.0,
                                      None, ALU.mult)
                    r = eng.scalar_tensor_tensor(
                        apr[:, hpar * 512:(hpar + 1) * 512],
                        pvsb[:, 2 * hpar * 512:(2 * hpar + 1) * 512], P40,
                        fbr[:, hpar * 512:(hpar + 1) * 512],
                        ALU.mult, ALU.subtract)
                    r.then_inc(s_epv, 1)
                    if p3_stage < 5:
                        continue
                    eng.wait_ge(s_epa, astage(hi, 3))
                    if hi >= 2:
                        eng.wait_ge(s_bc, bc_base(hi - 2) + 64)
                    eng.tensor_scalar(srow[r1:r1 + 1, 3 * 512:4 * 512],
                                      srow[r1:r1 + 1, 0:512], P40, None,
                                      ALU.mult)
                    eng.tensor_scalar(srow[r1:r1 + 1, 512:1024],
                                      srow[r1:r1 + 1, 4 * 512:5 * 512],
                                      1.0 / 128.0, None, ALU.mult)
                    eng.tensor_tensor(srow[r1:r1 + 1, 3 * 512:4 * 512],
                                      srow[r1:r1 + 1, 3 * 512:4 * 512],
                                      srow[r1:r1 + 1, 3 * 512:4 * 512],
                                      ALU.mult)
                    eng.tensor_scalar(vdum[:, 0:2], vdum[:, 0:2], 1.0,
                                      None, ALU.mult)
                    r = eng.scalar_tensor_tensor(
                        srow[r1:r1 + 1, 3 * 512:4 * 512],
                        srow[r1:r1 + 1, 3 * 512:4 * 512], 1e-5,
                        srow[r1:r1 + 1, 512:1024], ALU.mult, ALU.add)
                    r.then_inc(s_epv, 1)
                    if "arch" in dbg and hi in (0, 1):
                        eng.tensor_scalar(vdum[:, 0:2], vdum[:, 0:2], 1.0,
                                          None, ALU.mult)
                        eng.tensor_copy(
                            arch[r1:r1 + 1, 512:1024],
                            srow[r1:r1 + 1, 3 * 512:4 * 512])
                    if p3_stage < 6:
                        continue
                    eng.wait_ge(s_bc, bc_base(hi) + 80)
                    if qb >= 1:
                        eng.wait_ge(s_wo, 8 * qb)
                    r = eng.scalar_tensor_tensor(
                        aT[:, h * 512:(h + 1) * 512],
                        apr[:, hpar * 512:(hpar + 1) * 512], gp[:, 0:1],
                        rstdb[:, hpar * 512:(hpar + 1) * 512],
                        ALU.mult, ALU.mult)
                    r.then_inc(s_aT, 1)
                    if "arch" in dbg and hi in (0, 1):
                        eng.tensor_copy(
                            arch[r1:r1 + 1, 0:512],
                            srow[r1:r1 + 1, 3 * 512:4 * 512])
                if p3_h < H:
                    continue
                for sub in range(4):
                    tt = qb * 4 + sub
                    for half in range(2):
                        wn = qb * 8 + sub * 2 + half
                        eng.wait_ge(s_wo, wn + 1)
                        eng.wait_ge(s_xrl, 16 * (tt + 1))
                        if tt >= 1 and half == 0:
                            eng.wait_ge(s_ts, 16 * tt)
                        r = eng.tensor_tensor(
                            wo_t[:, half * 512:(half + 1) * 512],
                            xrl[:, half * 512:(half + 1) * 512],
                            PB[wn % 4][:], ALU.add)
                        r.then_inc(s_t4, 1)

        @blk.scalar
        def _(eng):
            for qb in range(p3_qb):
                ns = ns_of(qb)
                for h in range(p3_h):
                    hi = qb * H + h
                    hpar = hi % 2
                    scb = sc_base(qb, h)
                    mkb = mk_base(qb, h)
                    pvb = pv_base(qb, h)
                    r1 = 64 * hpar
                    for st in range(ns):
                        spar = st % 2
                        diag = st >= ns - 4
                        if hi == 0 and st == 0:
                            eng.wait_ge(s_bc, 48)
                        if diag:
                            eng.wait_ge(s_mk, mkb + 2 * (st - (ns - 4)) + 2)
                        else:
                            eng.wait_ge(s_sc, scb + 2 * st + 2)
                        if st >= 2 and p3_stage >= 2:
                            eng.wait_ge(s_pv, pvb + st - 1)
                        for br in range(2):
                            r = eng.activation(
                                ptile[:, (2 * spar + br) * 512:
                                      (2 * spar + br + 1) * 512],
                                PB[2 * spar + br][:], AF.Exp,
                                bias=bias16[:, 2 * h + br:2 * h + br + 1],
                                scale=0.125)
                            r.then_inc(s_exp, 1)
                    if p3_stage < 3:
                        continue
                    eng.wait_ge(s_pv, pvb + ns)
                    if hi >= 2:
                        eng.wait_ge(s_epv, dstage(hi - 2, 4))
                    eng.copy(pvsb[:, 2 * hpar * 512:(2 * hpar + 1) * 512],
                             PB[4][:])
                    r = eng.copy(pvsb[:, (2 * hpar + 1) * 512:
                                      (2 * hpar + 2) * 512], PB[5][:])
                    r.then_inc(s_pev, 2)
                    if p3_stage < 4:
                        continue
                    eng.wait_ge(s_epv, dstage(hi, 2))
                    r = eng.activation(srow[r1:r1 + 1, 512:1024],
                                       srow[r1:r1 + 1, 512:1024], AF.Ln)
                    afence(eng, r)
                    r = eng.activation(srow[r1:r1 + 1, 512:1024],
                                       srow[r1:r1 + 1, 512:1024],
                                       AF.Exp, scale=-1.0)
                    r.then_inc(s_epa, 1)
                    if p3_stage < 5:
                        continue
                    eng.wait_ge(s_epv, dstage(hi, 4))
                    r = eng.activation(sqa[:, hpar * 512:(hpar + 1) * 512],
                                       apr[:, hpar * 512:(hpar + 1) * 512],
                                       AF.Square)
                    r.then_inc(s_epa, 1)
                    eng.wait_ge(s_rms, hi + 1)
                    if hi >= 2:
                        eng.wait_ge(s_epv, dstage(hi - 2, 5))
                    r = eng.copy(srow[r1:r1 + 1, 4 * 512:5 * 512],
                                 PB[7][r1:r1 + 1, :])
                    r.then_inc(s_epa, 1)
                    eng.wait_ge(s_epv, dstage(hi, 5))
                    r = eng.activation(srow[r1:r1 + 1, 3 * 512:4 * 512],
                                       srow[r1:r1 + 1, 3 * 512:4 * 512], AF.Ln,
                                       scale=float(2.0 ** -40))
                    afence(eng, r)
                    r = eng.activation(srow[r1:r1 + 1, 3 * 512:4 * 512],
                                       srow[r1:r1 + 1, 3 * 512:4 * 512],
                                       AF.Exp, scale=-0.5,
                                       bias=cm20[r1:r1 + 1, 0:1])
                    r.then_inc(s_epa, 1)

        @blk.gpsimd
        def _(eng):
            eng.iota(srow[:, 0:512], [[1, 512]], channel_multiplier=-1,
                     allow_small_or_imprecise_dtypes=True).then_inc(s_p0, 1)
            for qb in range(p3_qb):
                for h in range(p3_h):
                    hi = qb * H + h
                    hpar = hi % 2
                    r1 = 64 * hpar
                    r2 = 64 * hpar + 32
                    b = bc_base(hi)
                    if p3_stage < 3:
                        continue
                    eng.wait_ge(s_epv, dstage(hi, 1))
                    if hi >= 2:
                        eng.wait_ge(s_epv, dstage(hi - 2, 5))
                        eng.wait_ge(s_epa, astage(hi - 2, 4))
                    eng.dma_start(srow[r1:r1 + 1, 512:1024],
                                  srow[r2:r2 + 1, 0:512]).then_inc(s_bc, 16)
                    if p3_stage < 4:
                        continue
                    eng.wait_ge(s_epv, dstage(hi, 3))
                    if hi >= 2:
                        eng.wait_ge(s_bc, bc_base(hi - 2) + 48)
                    eng.dma_start(bc_s[hpar * 2:hpar * 2 + 1, :],
                                  srow[r1:r1 + 1, 2 * 512:3 * 512]
                                  ).then_inc(s_bc, 16)
                    eng.wait_ge(s_bc, b + 32)
                    if hi >= 2:
                        eng.wait_ge(s_aT, hi - 1)
                    eng.dma_start(fbr[:, hpar * 512:(hpar + 1) * 512],
                                  bc_s[hpar * 2:hpar * 2 + 1, :]
                                  .broadcast_to([128, 512])).then_inc(s_bc, 16)
                    if p3_stage < 6:
                        continue
                    eng.wait_ge(s_epa, astage(hi, 4))
                    if hi >= 2:
                        eng.wait_ge(s_bc, bc_base(hi - 2) + 80)
                    eng.dma_start(bc_s[hpar * 2 + 1:hpar * 2 + 2, :],
                                  srow[r1:r1 + 1, 3 * 512:4 * 512]
                                  ).then_inc(s_bc, 16)
                    eng.wait_ge(s_bc, b + 64)
                    if hi >= 2:
                        eng.wait_ge(s_aT, hi - 1)
                    eng.dma_start(rstdb[:, hpar * 512:(hpar + 1) * 512],
                                  bc_s[hpar * 2 + 1:hpar * 2 + 2, :]
                                  .broadcast_to([128, 512])).then_inc(s_bc, 16)
                    if "p3trace" in dbg:
                        eng.wait_ge(s_aT, hi + 1)
                        eng.dma_start(dbg["p3trace"][hi:hi + 1, :],
                                      srow[r1:r1 + 1, 0:5 * 512]
                                      ).then_inc(s_dbg, 16)
                        ndbg[0] += 1
                        if hi < 4:
                            eng.dma_start(
                                dbg["pvd"][:, hi * 1024:(hi + 1) * 1024],
                                pvsb[:, 2 * hpar * 512:(2 * hpar + 2) * 512]
                                ).then_inc(s_dbg, 16)
                            ndbg[0] += 1
                    if "aTd" in dbg and h == H - 1:
                        eng.wait_ge(s_aT, hi + 1)
                        eng.dma_start(dbg["aTd"][:, qb * 4096:(qb + 1) * 4096],
                                      aT[:]).then_inc(s_dbg, 16)
                        ndbg[0] += 1
                if p3_h < H:
                    continue
                for sub in range(4):
                    tt = qb * 4 + sub
                    if tt >= 1:
                        eng.wait_ge(s_t4, 2 * tt)
                    eng.dma_start(xrl[:, :],
                                  x_d[tt * 128:(tt + 1) * 128, :]
                                  ).then_inc(s_xrl, 16)
                    eng.wait_ge(s_t4, 2 * tt + 2)
                    eng.dma_start(ts_s[tt * 128:(tt + 1) * 128, :],
                                  wo_t[:, :]
                                  ).then_inc(s_ts, 16)

        if max_phase <= 3:
            if p3_qb >= NQB and p3_h >= H:
                fw = [(s_ts, 16 * NT), (s_aT, NHEADS)]
            elif p3_h >= H:
                fw = [(s_ts, 16 * 4 * p3_qb), (s_aT, 8 * p3_qb)]
            else:
                if p3_stage >= 6:
                    fw = [(s_aT, p3_h)]
                elif p3_stage >= 5:
                    fw = [(s_epv, 5 * (p3_h - 1) + 5), (s_epa, 4 * (p3_h - 1) + 4)]
                elif p3_stage >= 4:
                    fw = [(s_epv, 5 * (p3_h - 1) + 4)]
                elif p3_stage >= 3:
                    fw = [(s_epv, 5 * (p3_h - 1) + 2)]
                elif p3_stage >= 2:
                    fw = [(s_pv, pv_base(0, p3_h - 1) + 4)]
                else:
                    fw = [(s_exp, sc_base(0, p3_h - 1) + 8)]
            dl = [("tattn", ts_s[:, :]), ("lam", lam[:])]
            if arch is not None:
                dl.append(("arch", arch[:]))
            _finisher(fw, dl)
            ctxD.close()
            ctxB.close()
            return nc, dbg

        ctxD.close()
        ctxB.close()
        wdqT = sbC("wdqT", [128, ICH * E], BF16)
        wgT = sbC("wgT", [128, 2 * KCH * 512], BF16)
        wuT = sbC("wuT", [128, 2 * KCH * 512], BF16)
        sg = sbC("sg", [128, 2 * 512], F32)
        hblk = sbC("hblk", [128, 2 * 512], F32)
        hqb = sbC("hqb", [128, 2 * 512], BF16)
        hqT = sbC("hqT", [128, TOKGRP * ICH * 128], BF16)
        trl = sbC("trl", [128, 2 * E], F32)
        yt = sbC("yt", [128, 2 * E], F32)
        ws4 = mkws("ws4_", 512, sbC)

        @blk.sync
        def _(eng):
            eng.wait_ge(s_ts, 16 * NT)
            eng.wait_ge(s_wqo, 16 * NWCHUNK)
            for ic in range(ICH):
                eng.dma_start(wdqT[:, ic * E:(ic + 1) * E],
                              wdq_s[:, ic * 128:(ic + 1) * 128],
                              transpose=True).then_inc(s_wdT, 16)
            for G in range(NGRP):
                for ib in range(NIB):
                    bi = G * NIB + ib
                    p = bi % 2
                    sz = IBLK[ib]
                    i0 = IB_OFF[ib]
                    if bi >= 2:
                        eng.wait_ge(s_gu, 16 * (bi - 1))
                    for k in range(KCH):
                        eng.dma_start(
                            wgT[:, (p * KCH + k) * 512:(p * KCH + k) * 512 + sz],
                            wgq_s[i0:i0 + sz, k * 128:(k + 1) * 128],
                            transpose=True).then_inc(s_wgT, 16)
                        eng.dma_start(
                            wuT[:, (p * KCH + k) * 512:(p * KCH + k) * 512 + sz],
                            wuq_s[i0:i0 + sz, k * 128:(k + 1) * 128],
                            transpose=True).then_inc(s_wgT, 16)

        @blk.tensor
        def _(eng):
            pend = []
            gu_done = 0

            def flush_transposes(upto):
                while pend and pend[0][0] <= upto:
                    gi, tg, sz = pend.pop(0)
                    nbk = sz // 128
                    eng.wait_ge(s_hq, gi + 1)
                    if gi >= 2:
                        eng.wait_ge(s_hev, gi - 1)
                    v = PB[4 + tg % 2][:].bitcast(BF16)
                    for j in range(nbk):
                        r = eng.transpose(v[:, j * 128:(j + 1) * 128],
                                          hqb[:, (tg % 2) * 512 + j * 128:
                                              (tg % 2) * 512 + (j + 1) * 128],
                                          iden[:])
                    r.then_inc(s_htr, 1)

            first = True
            for G in range(NGRP):
                for ib in range(NIB):
                    bi = G * NIB + ib
                    p = bi % 2
                    sz = IBLK[ib]
                    for tg in range(TOKGRP):
                        t = G * TOKGRP + tg
                        gi = bi * TOKGRP + tg
                        flush_transposes(gi - 2)
                        if tg > 0:
                            eng.wait_ge(s_wgT, 16 * 16 * (bi + 1))
                        if first:
                            eng.wait_ge(s_t4, 32)
                            eng.wait_ge(s_pev, 64)
                            first = False
                        if gi >= 2:
                            eng.wait_ge(s_si, gi - 1)
                            eng.wait_ge(s_h, gi - 1)
                        for k in range(KCH):
                            if tg == 0:
                                eng.wait_ge(s_wgT,
                                            16 * (16 * bi + 2 * k + 1))
                            r = eng.matmul(
                                PB[tg % 2][:, 0:sz],
                                xqT[:, k * T + t * 128:k * T + (t + 1) * 128],
                                wgT[:, (p * KCH + k) * 512:
                                    (p * KCH + k) * 512 + sz],
                                start=(k == 0), stop=(k == KCH - 1))
                        r.then_inc(s_gu, 1)
                        for k in range(KCH):
                            if tg == 0:
                                eng.wait_ge(s_wgT,
                                            16 * (16 * bi + 2 * k + 2))
                            r = eng.matmul(
                                PB[2 + tg % 2][:, 0:sz],
                                xqT[:, k * T + t * 128:k * T + (t + 1) * 128],
                                wuT[:, (p * KCH + k) * 512:
                                    (p * KCH + k) * 512 + sz],
                                start=(k == 0), stop=(k == KCH - 1))
                        r.then_inc(s_gu, 1)
                        pend.append((gi, tg, sz))
                flush_transposes(10 ** 9)
                for tg in range(TOKGRP):
                    dn = G * TOKGRP + tg
                    eng.wait_ge(s_hev,
                                G * NIB * TOKGRP + (NIB - 1) * TOKGRP + tg + 1)
                    eng.wait_ge(s_wdT, 16 * ICH)
                    if dn == 0:
                        eng.wait_ge(s_epv, 5 * NHEADS)
                        eng.wait_ge(s_epa, 4 * NHEADS)
                    if dn >= 1:
                        eng.wait_ge(s_y, 2 * (dn - 1) + 2)
                    for half in range(2):
                        for ic in range(ICH):
                            r = eng.matmul(
                                PB[6 + half][:],
                                hqT[:, (tg * ICH + ic) * 128:
                                    (tg * ICH + ic + 1) * 128],
                                wdqT[:, ic * E + half * 512:
                                     ic * E + (half + 1) * 512],
                                start=(ic == 0), stop=(ic == ICH - 1))
                        r.then_inc(s_dn, 1)

        @blk.scalar
        def _(eng):
            for G in range(NGRP):
                for ib in range(NIB):
                    for tg in range(TOKGRP):
                        gi = (G * NIB + ib) * TOKGRP + tg
                        sz = IBLK[ib]
                        eng.wait_ge(s_gu, 2 * gi + 1)
                        if gi >= 2:
                            eng.wait_ge(s_h, gi - 1)
                        r = eng.activation(sg[:, (tg % 2) * 512:
                                               (tg % 2) * 512 + sz],
                                           PB[tg % 2][:, 0:sz], AF.Silu)
                        r.then_inc(s_si, 1)

        @blk.vector
        def _(eng):
            il = VecEmitter(eng, vdum)
            for G in range(NGRP):
                for ib in range(NIB):
                    sz = IBLK[ib]
                    i0 = IB_OFF[ib]
                    for tg in range(TOKGRP):
                        gi = (G * NIB + ib) * TOKGRP + tg
                        p = tg % 2
                        ths = []
                        ths.append((lambda gi=gi: eng.wait_ge(s_si, gi + 1),
                                    "w"))
                        ths.append((lambda gi=gi: eng.wait_ge(s_gu,
                                                              2 * gi + 2),
                                    "w"))

                        def hmul(p=p, sz=sz):
                            r = eng.tensor_tensor(
                                hblk[:, p * 512:p * 512 + sz],
                                sg[:, p * 512:p * 512 + sz],
                                PB[2 + p][:, 0:sz], ALU.mult)
                            r.then_inc(s_h, 1)
                            return r
                        ths.append((hmul, False))
                        if gi >= 2:
                            ths.append((lambda gi=gi: eng.wait_ge(
                                s_htr, gi - 1), "w"))
                        qt = quant_thunks(eng, hblk[:, p * 512:p * 512 + sz],
                                          hqb[:, p * 512:p * 512 + sz],
                                          ws4, sz)
                        last_t, last_f = qt[-1]

                        def qfin(last_t=last_t):
                            r = last_t()
                            r.then_inc(s_hq, 1)
                            return r
                        qt[-1] = (qfin, last_f)
                        ths.extend(qt)

                        def evac(gi=gi, tg=tg, i0=i0, sz=sz, p=p, G=G):
                            eng.wait_ge(s_htr, gi + 1)
                            if G >= 1:
                                eng.wait_ge(s_dn, G * TOKGRP * 2)
                            v = PB[4 + p][:].bitcast(BF16)
                            nbk = sz // 128
                            for j in range(nbk):
                                ic = i0 // 128 + j
                                r = eng.tensor_copy(
                                    hqT[:, (tg * ICH + ic) * 128:
                                        (tg * ICH + ic + 1) * 128],
                                    v[:, j * 128:(j + 1) * 128])
                            r.then_inc(s_hev, 1)
                            return r
                        ths.append((evac, False))
                        il.run([("h", ths)])
                for tg in range(TOKGRP):
                    dn = G * TOKGRP + tg
                    for half in range(2):
                        eng.wait_ge(s_dn, 2 * dn + half + 1)
                        eng.wait_ge(s_trl, 16 * (dn + 1))
                        if dn >= 2 and half == 0:
                            eng.wait_ge(s_out, 16 * (dn - 1))
                        r = eng.tensor_tensor(
                            yt[:, (tg % 2) * E + half * 512:
                               (tg % 2) * E + (half + 1) * 512],
                            trl[:, (tg % 2) * E + half * 512:
                                (tg % 2) * E + (half + 1) * 512],
                            PB[6 + half][:], ALU.add)
                        r.then_inc(s_y, 1)

        @blk.gpsimd
        def _(eng):
            for G in range(NGRP):
                for tg in range(TOKGRP):
                    t = G * TOKGRP + tg
                    dn = G * TOKGRP + tg
                    eng.wait_ge(s_ts, 16 * (t + 1))
                    if dn >= 2:
                        eng.wait_ge(s_y, 2 * (dn - 2) + 2)
                    eng.dma_start(trl[:, (tg % 2) * E:(tg % 2 + 1) * E],
                                  ts_s[t * 128:(t + 1) * 128, :]
                                  ).then_inc(s_trl, 16)
                    eng.wait_ge(s_y, 2 * dn + 2)
                    eng.dma_start(y_d[t * 128:(t + 1) * 128, :],
                                  yt[:, (tg % 2) * E:(tg % 2 + 1) * E]
                                  ).then_inc(s_out, 16)
            eng.wait_ge(s_out, 16 * NT)
            pairs = [("xqT", xqT), ("lam", lam)]
            for name, tile in pairs:
                if name in dbg:
                    eng.dma_start(dbg[name][:, :], tile[:]).then_inc(s_dbg, 16)
                    ndbg[0] += 1
            drams = [("tattn", ts_s), ("wgq", wgq_s), ("wuq", wuq_s),
                     ("wdq", wdq_s)]
            for name, dt_ in drams:
                if name in dbg:
                    eng.dma_start(dbg[name][:, :], dt_[:, :]).then_inc(s_dbg, 16)
                    ndbg[0] += 1
            if ndbg[0]:
                eng.wait_ge(s_dbg, 16 * ndbg[0])

        ctxC.close()

    return nc, dbg


def kernel(**inputs):
    x_full = np.asarray(inputs["L0_attn_x"], dtype=np.float32)
    base = {
        "wq": np.ascontiguousarray(inputs["L0_attn_wq"], dtype=np.float32),
        "wk": np.ascontiguousarray(inputs["L0_attn_wk"], dtype=np.float32),
        "wv": np.ascontiguousarray(inputs["L0_attn_wv"], dtype=np.float32),
        "wo": np.ascontiguousarray(inputs["L0_attn_wo"], dtype=np.float32),
        "lq1": np.ascontiguousarray(inputs["L0_attn_lambda_q1"], dtype=np.float32),
        "lk1": np.ascontiguousarray(inputs["L0_attn_lambda_k1"], dtype=np.float32),
        "lq2": np.ascontiguousarray(inputs["L0_attn_lambda_q2"], dtype=np.float32),
        "lk2": np.ascontiguousarray(inputs["L0_attn_lambda_k2"], dtype=np.float32),
        "g": np.ascontiguousarray(inputs["L0_attn_g"], dtype=np.float32),
        "wg": np.ascontiguousarray(inputs["L0_ffn_w_gate"], dtype=np.float32),
        "wu": np.ascontiguousarray(inputs["L0_ffn_w_up"], dtype=np.float32),
        "wd": np.ascontiguousarray(inputs["L0_ffn_w_down"], dtype=np.float32),
    }
    nc, dbg = build_program(debug=_DEBUG)
    in_maps = []
    for c in range(8):
        m = dict(base)
        m["x"] = np.ascontiguousarray(x_full[c])
        in_maps.append(m)
    res = run_bass_kernel_spmd(nc, in_maps, core_ids=list(range(8)))
    out = np.stack([np.asarray(res.results[c]["y"], dtype=np.float32)
                    for c in range(8)], axis=0)
    if _DEBUG:
        kernel.last_results = res
    return out
